# version 05
# baseline (speedup 1.0000x reference)
import numpy as np
import ml_dtypes
import concourse.bass as bass
import concourse.mybir as mybir
from concourse.bass_utils import run_bass_kernel_spmd

F32 = mybir.dt.float32
BF16 = mybir.dt.bfloat16
AF = mybir.ActivationFunctionType
AL = mybir.AluOpType
AX = mybir.AxisListType
BFNP = ml_dtypes.bfloat16

NCORES = 8
ENG = ("tensor", "scalar", "vector", "gpsimd", "sync")

CFG = dict(B=2, S=8192, D=2048, E=8, FF=512)


def _derive(cfg):
    d = dict(cfg)
    B, S, D, E, FF = d["B"], d["S"], d["D"], d["E"], d["FF"]
    d["H"] = D // 128
    d["DC"] = D // 128
    d["FFC"] = FF // 128
    d["OB"] = S // NCORES
    d["QT"] = min(512, d["OB"])
    d["NQT"] = d["OB"] // d["QT"]
    d["TOWN"] = B * d["OB"]
    d["TT"] = d["OB"] // 2
    d["NTT"] = S // d["TT"]
    d["TSW"] = min(512, d["TOWN"])
    d["NTS"] = d["TOWN"] // d["TSW"]
    d["NBAND"] = d["QT"] // 16
    assert d["OB"] == d["QT"] * d["NQT"] and d["TT"] % 128 == 0
    return d


class Arena:
    def __init__(self, tile, cols):
        self.tile, self.cols, self.off, self.stack = tile, cols, 0, []
        self.hwm = 0

    def alloc(self, n):
        assert self.off + n <= self.cols, f"arena overflow {self.off}+{n}>{self.cols}"
        off = self.off
        self.off += n
        self.hwm = max(self.hwm, self.off)
        return off

    def push(self):
        self.stack.append(self.off)

    def pop(self):
        self.off = self.stack.pop()

    def sl(self, off, n, p0=0, p1=128):
        return self.tile[p0:p1, off:off + n]


class Em:
    def __init__(self):
        self.prog = {e: [] for e in ENG}
        self.cnt = {n: 0 for n in ("t", "s", "v", "g", "gd", "sd", "bar")}
        self.phase = "init"

    def op(self, eng, fn, waits=(), inc=None, amt=1):
        w = tuple((sn, val) for (sn, val) in waits if val)
        self.prog[eng].append((w, fn, inc, amt, self.phase))
        if inc is not None:
            self.cnt[inc] += amt
            return self.cnt[inc]
        return None

    def check(self):
        tot = dict(self.cnt)
        sem = {n: 0 for n in tot}
        ptr = {e: 0 for e in ENG}
        for e in ENG:
            for w, _, _, _, _ in self.prog[e]:
                for sn, val in w:
                    assert val <= tot[sn], f"{e}: wait {sn}>={val} > total {tot[sn]}"
        progressed = True
        while progressed:
            progressed = False
            for e in ENG:
                while ptr[e] < len(self.prog[e]):
                    w, _, inc, amt, _ = self.prog[e][ptr[e]]
                    if all(sem[sn] >= val for sn, val in w):
                        if inc is not None:
                            sem[inc] += amt
                        ptr[e] += 1
                        progressed = True
                    else:
                        break
        stuck = {e: (ptr[e], len(self.prog[e])) for e in ENG if ptr[e] < len(self.prog[e])}
        assert not stuck, f"schedule deadlock at {stuck}"


def build(cfg):
    dv = _derive(cfg)
    B, S, D, NE, FF = dv["B"], dv["S"], dv["D"], dv["E"], dv["FF"]
    H, DC, FFC = dv["H"], dv["DC"], dv["FFC"]
    OB, QT, NQT, TOWN = dv["OB"], dv["QT"], dv["NQT"], dv["TOWN"]
    TT, NTT, TSW, NTS, NBAND = dv["TT"], dv["NTT"], dv["TSW"], dv["NTS"], dv["NBAND"]
    KVF = 256
    NQKV = D + KVF
    SCALE = float(np.float32(1.0 / np.sqrt(128.0)))

    nc = bass.Bass("TRN2", target_bir_lowering=False)

    xT = nc.dram_tensor("xT", [B, D, S], F32, kind="ExternalInput").ap()
    xTown = nc.dram_tensor("xTown", [D, TOWN], F32, kind="ExternalInput").ap()
    wqkvT = nc.dram_tensor("wqkvT", [D, NQKV], F32, kind="ExternalInput").ap()
    wqkvb = nc.dram_tensor("wqkvb", [128, NQKV // 128], F32, kind="ExternalInput").ap()
    outwT = nc.dram_tensor("outwT", [D, D], F32, kind="ExternalInput").ap()
    outb = nc.dram_tensor("outb", [128, DC], F32, kind="ExternalInput").ap()
    routerT = nc.dram_tensor("routerT", [D, NE], F32, kind="ExternalInput").ap()
    w1 = nc.dram_tensor("w1", [NE, D, FF], F32, kind="ExternalInput").ap()
    w3 = nc.dram_tensor("w3", [NE, D, FF], F32, kind="ExternalInput").ap()
    w2 = nc.dram_tensor("w2", [NE, FF, D], F32, kind="ExternalInput").ap()
    identf = nc.dram_tensor("identf", [128, 128], F32, kind="ExternalInput").ap()
    onesbf = nc.dram_tensor("onesbf", [128, 128], F32, kind="ExternalInput").ap()
    stairbf = nc.dram_tensor("stairbf", [128, 16], F32, kind="ExternalInput").ap()
    eselbf = nc.dram_tensor("eselbf", [8, NE * 128], F32, kind="ExternalInput").ap()
    cstf = nc.dram_tensor("cstf", [128, 8], F32, kind="ExternalInput").ap()
    out = nc.dram_tensor("out", [D, TOWN], F32, kind="ExternalOutput").ap()

    em = Em()
    C = em.cnt

    import contextlib
    with contextlib.ExitStack() as _es:
        _e = _es.enter_context
        ABt = _e(nc.sbuf_tensor("AB", [128, 91500], BF16))
        AFt = _e(nc.sbuf_tensor("AF", [128, 6600], F32))
        sb_identf = _e(nc.sbuf_tensor("sb_identf", [128, 128], F32))
        sb_ones = _e(nc.sbuf_tensor("sb_ones", [128, 128], BF16))
        sb_stair = _e(nc.sbuf_tensor("sb_stair", [128, 16], BF16))
        sb_qkvb = _e(nc.sbuf_tensor("sb_qkvb", [128, NQKV // 128], F32))
        sb_outb = _e(nc.sbuf_tensor("sb_outb", [128, DC], F32))
        sb_cst = _e(nc.sbuf_tensor("sb_cst", [128, 8], F32))
        sem_t = _e(nc.semaphore("sem_t"))
        sem_s = _e(nc.semaphore("sem_s"))
        sem_v = _e(nc.semaphore("sem_v"))
        sem_g = _e(nc.semaphore("sem_g"))
        sem_gd = _e(nc.semaphore("sem_gd"))
        sem_sd = _e(nc.semaphore("sem_sd"))
        sem_bar = _e(nc.semaphore("sem_bar"))
        PS = _e(nc.psum_tensor("PS", [128, 4096], F32))
        blk = _e(nc.Block())
        sems = {"t": sem_t, "s": sem_s, "v": sem_v, "g": sem_g,
                "gd": sem_gd, "sd": sem_sd, "bar": sem_bar}
        AB = Arena(ABt, 91500)
        FA = Arena(AFt, 6600)
        bank_w = {i: [] for i in range(8)}

        def bank(i, w=512, rows=(0, 128)):
            return PS[rows[0]:rows[1], i * 512:i * 512 + w]

        def barrier():
            tgt = C["bar"] + 5
            for e in ENG:
                em.op(e, lambda E: E.drain(), inc="bar")
            for e in ENG:
                em.op(e, None, waits=[("bar", tgt)])
            for i in range(8):
                bank_w[i] = []

        K_ONE, K_EPS, K_INVD, K_NEG30 = 0, 1, 2, 3

        em.phase = "consts"
        em.op("sync", lambda E: E.dma_start(sb_identf[:, :], identf[:, :]), inc="sd", amt=16)
        em.op("gpsimd", lambda E: E.dma_start(sb_ones[:, :], onesbf[:, :]), inc="gd", amt=16)
        em.op("gpsimd", lambda E: E.dma_start(sb_stair[:, :], stairbf[:, :]), inc="gd", amt=16)
        em.op("sync", lambda E: E.dma_start(sb_qkvb[:, :], wqkvb[:, :]), inc="sd", amt=16)
        em.op("sync", lambda E: E.dma_start(sb_outb[:, :], outb[:, :]), inc="sd", amt=16)
        cm = em.op("sync", lambda E: E.dma_start(sb_cst[:, :], cstf[:, :]), inc="sd", amt=16)
        o_x1 = AB.alloc(DC * TOWN)
        o_gates = AB.alloc(TOWN)
        o_rtr = AB.alloc(DC * NE)
        o_esel = AB.alloc(NE * 128)
        em.op("gpsimd", lambda E: E.dma_start(
            AB.sl(o_rtr, DC * NE).rearrange("p (c e) -> p c e", c=DC),
            routerT.rearrange("(c p) e -> p c e", p=128)), inc="gd", amt=16)
        gm0 = em.op("gpsimd", lambda E: E.dma_start(
            AB.sl(o_esel, NE * 128, 0, 8), eselbf[:, :]), inc="gd", amt=16)
        for e in ENG:
            em.op(e, None, waits=[("sd", cm), ("gd", gm0)])

        ones_col = sb_ones[:, 0:1]
        ones_row = sb_ones[0:1, :]

        def x1sl(c, o0, n):
            return AB.sl(o_x1 + c * TOWN + o0, n)

        def ln_rows(n_t, W, o_rows, o_scr, stat_wait, stat_bank, sq_bank,
                    o_mrow, o_rrow):
            dmk, fmk = {}, {}
            for t in range(n_t):
                def r0(E, t=t):
                    return E.tensor_scalar(
                        FA.sl(o_rows + t * 1024, W, 0, 1),
                        bank(stat_bank(t), W, rows=(0, 1)),
                        sb_cst[0:1, K_INVD:K_INVD + 1], None, AL.mult)
                em.op("vector", r0, waits=([("t", stat_wait)] if t == 0 else []))
            vr1 = None
            for t in range(n_t):
                def r1(E, t=t):
                    return E.tensor_scalar(
                        FA.sl(o_rows + t * 1024 + 512, W, 0, 1),
                        bank(sq_bank(t), W, rows=(0, 1)),
                        sb_cst[0:1, K_INVD:K_INVD + 1], None, AL.mult)
                vr1 = em.op("vector", r1, inc="v")
            for t in range(n_t):
                w_ = [("v", vr1)] if t == 0 else []
                if t - 2 in dmk:
                    w_.append(("v", dmk[t - 2]))
                def rq(E, t=t):
                    return E.activation(FA.sl(o_scr + (t % 2) * 512, W, 0, 1),
                                        FA.sl(o_rows + t * 1024, W, 0, 1),
                                        AF.Square, scale=1.0)
                em.op("scalar", rq, waits=w_, inc="s")
                def rd_(E, t=t):
                    r = FA.sl(o_rows + t * 1024 + 512, W, 0, 1)
                    return E.tensor_tensor(r, r, FA.sl(o_scr + (t % 2) * 512, W, 0, 1),
                                           AL.subtract)
                dmk[t] = em.op("vector", rd_, waits=[("s", C["s"])], inc="v")
            for t in range(n_t):
                w_ = [("v", dmk[t])]
                if t - 2 in fmk:
                    w_.append(("v", fmk[t - 2]))
                def rs_(E, t=t):
                    return E.activation(FA.sl(o_scr + (t % 2) * 512, W, 0, 1),
                                        FA.sl(o_rows + t * 1024 + 512, W, 0, 1),
                                        AF.Sqrt, bias=sb_cst[0:1, K_EPS:K_EPS + 1],
                                        scale=1.0)
                em.op("scalar", rs_, waits=w_, inc="s")
                def rr_(E, t=t):
                    return E.reciprocal(FA.sl(o_rows + t * 1024 + 512, W, 0, 1),
                                        FA.sl(o_scr + (t % 2) * 512, W, 0, 1))
                fmk[t] = em.op("vector", rr_, waits=[("s", C["s"])], inc="v")
            cpk = {}
            for t in range(n_t):
                def c1(E, t=t):
                    return E.tensor_copy(AB.sl(o_mrow + t * 512, W, 0, 1),
                                         FA.sl(o_rows + t * 1024, W, 0, 1))
                em.op("vector", c1)
                def c2(E, t=t):
                    return E.tensor_copy(AB.sl(o_rrow + t * 512, W, 0, 1),
                                         FA.sl(o_rows + t * 1024 + 512, W, 0, 1))
                cpk[t] = em.op("vector", c2, inc="v")
            return cpk



        for b in range(B):
            AB.push()
            o_qct = AB.alloc(H * OB)
            AB.push()
            o_kT = AB.alloc(S)
            o_vKM = AB.alloc(S)
            o_P = AB.alloc(4 * 1024)
            o_A0 = AB.alloc(2 * 512)
            o_A1 = AB.alloc(2 * 512)
            o_dr = AB.alloc(2 * 512)
            o_vsp = AB.alloc(16)
            FA.push()
            o_Bsb = FA.alloc(2 * 512)
            o_Rsb = FA.alloc(2 * 512)
            o_vtf = FA.alloc(2 * TT)

            def qrhs(h, j, c0):
                return AB.sl(o_qct + h * OB + j * QT + c0, QT - c0)

            em.phase = f"kvq{b}"
            AB.push()
            o_Wkv = AB.alloc(DC * KVF)
            o_wq = AB.alloc(2 * DC * 128)
            wkvm = em.op("gpsimd", lambda E: E.dma_start(
                AB.sl(o_Wkv, DC * KVF).rearrange("p (c f) -> p c f", c=DC),
                wqkvT[:, D:D + KVF].rearrange("(c p) f -> p c f", p=128)),
                inc="gd", amt=16)

            xt_mark = {}
            vt_tmark = {}
            NTR = TT // 128
            for tt_i in range(NTT):
                slot = tt_i % 2
                waits = [("gd", wkvm)] + ([xt_mark[tt_i - 2]] if tt_i >= 2 else [])
                def dxt(E, tt_i=tt_i, slot=slot, b=b):
                    d3 = ABt[:, o_x1:o_x1 + DC * TOWN].rearrange(
                        "p (c t) -> p c t", c=DC)[:, :, b * OB + slot * TT:
                                                  b * OB + slot * TT + TT]
                    s3 = xT[b][:, tt_i * TT:(tt_i + 1) * TT].rearrange(
                        "(c p) t -> p c t", p=128)
                    return E.dma_start(d3, s3)
                dm = em.op("gpsimd", dxt, waits=waits, inc="gd", amt=16)
                kb, vb = tt_i % 2, 2 + tt_i % 2
                mmk = {}
                for fi, bnk in ((0, kb), (1, vb)):
                    for dc in range(DC):
                        w_ = ([("gd", dm)] + bank_w[bnk]) if dc == 0 else []
                        def mm(E, fi=fi, bnk=bnk, dc=dc, slot=slot, b=b):
                            return E.matmul(
                                bank(bnk, TT), AB.sl(o_Wkv + dc * KVF + fi * 128, 128),
                                x1sl(dc, b * OB + slot * TT, TT),
                                start=(dc == 0), stop=(dc == DC - 1))
                        mk = em.op("tensor", mm, waits=w_,
                                   inc=("t" if dc == DC - 1 else None))
                    mmk[fi] = mk
                xt_mark[tt_i] = ("t", mmk[1])
                def evK(E, tt_i=tt_i, kb=kb):
                    return E.activation(AB.sl(o_kT + tt_i * TT, TT), bank(kb, TT),
                                        AF.Identity, bias=sb_qkvb[:, H:H + 1], scale=1.0)
                sm1 = em.op("scalar", evK, waits=[("t", mmk[0])], inc="s")
                bank_w[kb] = [("s", sm1)]
                wv = [("t", mmk[1])] + ([vt_tmark[tt_i - 2]] if tt_i >= 2 else [])
                def evV(E, tt_i=tt_i, vb=vb, slot=slot):
                    return E.activation(FA.sl(o_vtf + slot * TT, TT), bank(vb, TT),
                                        AF.Identity, bias=sb_qkvb[:, H + 1:H + 2], scale=1.0)
                sm2 = em.op("scalar", evV, waits=wv, inc="s")
                bank_w[vb] = [("s", sm2)]
                for q in range(NTR):
                    tb = 4 + (tt_i * NTR + q) % 2
                    def tr(E, q=q, tb=tb, slot=slot):
                        return E.transpose(bank(tb, 128),
                                           FA.sl(o_vtf + slot * TT + q * 128, 128),
                                           sb_identf[:, :])
                    tm = em.op("tensor", tr, waits=[("s", sm2)] + bank_w[tb], inc="t")
                    vt_tmark[tt_i] = ("t", tm)
                    def cp(E, tt_i=tt_i, q=q, tb=tb):
                        return E.tensor_copy(
                            AB.sl(o_vKM + (tt_i * NTR + q) * 128, 128), bank(tb, 128))
                    vm = em.op("vector", cp, waits=[("t", tm)], inc="v")
                    bank_w[tb] = [("v", vm)]

            qx_dma = {}
            kv_all = ("t", C["t"])
            for tq in range(NQT):
                def dqx(E, tq=tq, b=b):
                    d3 = ABt[:, o_x1:o_x1 + DC * TOWN].rearrange(
                        "p (c t) -> p c t", c=DC)[:, :, b * OB + tq * QT:
                                                  b * OB + (tq + 1) * QT]
                    s3 = xTown[:, b * OB + tq * QT:b * OB + (tq + 1) * QT].rearrange(
                        "(c p) t -> p c t", p=128)
                    return E.dma_start(d3, s3)
                qx_dma[tq] = em.op("gpsimd", dqx, waits=[kv_all], inc="gd", amt=16)
            wq_mark = {}
            for fc in range(H):
                slot = fc % 2
                w_ = [wq_mark[fc - 2]] if fc >= 2 else []
                def dwq(E, fc=fc, slot=slot):
                    return E.dma_start(
                        AB.sl(o_wq + slot * DC * 128, DC * 128).rearrange(
                            "p (c f) -> p c f", c=DC),
                        wqkvT[:, fc * 128:(fc + 1) * 128].rearrange(
                            "(c p) f -> p c f", p=128))
                dm = em.op("gpsimd", dwq, waits=w_, inc="gd", amt=16)
                for tq in range(NQT):
                    bnk = (fc * NQT + tq) % 4
                    for dc in range(DC):
                        w_ = ([("gd", dm), ("gd", qx_dma[tq])] + bank_w[bnk]) if dc == 0 else []
                        def mmq(E, fc=fc, tq=tq, dc=dc, slot=slot, bnk=bnk, b=b):
                            return E.matmul(
                                bank(bnk, QT),
                                AB.sl(o_wq + slot * DC * 128 + dc * 128, 128),
                                x1sl(dc, b * OB + tq * QT, QT),
                                start=(dc == 0), stop=(dc == DC - 1))
                        qm = em.op("tensor", mmq, waits=w_,
                                   inc=("t" if dc == DC - 1 else None))
                    wq_mark[fc] = ("t", qm)
                    def evQ(E, fc=fc, tq=tq, bnk=bnk):
                        return E.activation(
                            AB.sl(o_qct + fc * OB + tq * QT, QT), bank(bnk, QT),
                            AF.Identity, bias=sb_qkvb[:, fc:fc + 1], scale=1.0)
                    sm = em.op("scalar", evQ, waits=[("t", qm)], inc="s")
                    bank_w[bnk] = [("s", sm)]
            AB.pop()
            barrier()

            em.phase = f"attn{b}"
            slot_last_exp = {0: 0, 1: 0}
            P_hist = {(s, p): None for s in (0, 1) for p in (0, 1)}
            ctx_free = {}
            d_free = {}
            prev_last_ones = {0: 0, 1: 0}
            for j in range(NQT):
                M = QT * (j + 1) // 16
                NP = M // 2

                def c0f(m, M=M):
                    return max(0, 16 * (m - (M - NBAND)))

                for hp in range(H // 2):
                    heads = (2 * hp, 2 * hp + 1)
                    st = {}
                    for s in (0, 1):
                        st[s] = dict(pv_mark={}, pair_v={}, pair_g={}, ones_mark={},
                                     h=heads[s], cbank=4 + s, dbank=6 + s)

                    def emit_pv(s, g, j=j, M=M, st=st):
                        d = st[s]
                        mk = None
                        for ii, m in enumerate((2 * g, 2 * g + 1)):
                            first, last = (m == 0), (m == M - 1)
                            w_ = []
                            if ii == 0:
                                w_ = [("v", d["pair_v"][g])]
                                if first and ctx_free.get(s):
                                    w_.append(("v", ctx_free[s]))
                            cc = c0f(m)
                            def pv(E, m=m, g=g, cc=cc, first=first, last=last,
                                   s=s, d=d):
                                off = 512 if (m % 2) else cc
                                return E.matmul(
                                    PS[:, d["cbank"] * 512 + cc: d["cbank"] * 512 + QT],
                                    AB.sl(o_vKM + m * 128, 128),
                                    AB.sl(o_P + (2 * s + g % 2) * 1024 + off, QT - cc),
                                    start=first, stop=last)
                            mk = em.op("tensor", pv, waits=w_,
                                       inc=("t" if ii == 1 else None)) or mk
                        d["pv_mark"][g] = mk
                        P_hist[(s, g % 2)] = (mk, d["pair_v"][g], d["pair_g"][g])
                        return mk

                    def emit_ones(s, G, glast, st=st):
                        d = st[s]
                        ccg0, ccg1 = c0f(8 * G), c0f(8 * G + 1)
                        w_ = [("v", d["pair_v"][4 * G + 3]), ("g", d["pair_g"][4 * G + 3])]
                        if G == 0 and d_free.get(s):
                            w_.append(("s", d_free[s]))
                        def o1(E, G=G, ccg0=ccg0, s=s, d=d):
                            return E.matmul(
                                PS[0:1, d["dbank"] * 512 + ccg0: d["dbank"] * 512 + QT],
                                ones_col, AB.sl(o_A0 + s * 512 + ccg0, QT - ccg0),
                                start=(G == 0), stop=False)
                        em.op("tensor", o1, waits=w_)
                        def o2(E, glast=glast, ccg1=ccg1, s=s, d=d):
                            return E.matmul(
                                PS[0:1, d["dbank"] * 512 + ccg1: d["dbank"] * 512 + QT],
                                ones_col, AB.sl(o_A1 + s * 512 + ccg1, QT - ccg1),
                                start=False, stop=glast)
                        mk = em.op("tensor", o2, inc="t")
                        d["ones_mark"][G] = mk
                        return mk

                    def emit_gstep(g):
                        m0, m1 = 2 * g, 2 * g + 1
                        cc0, cc1 = c0f(m0), c0f(m1)
                        emks = {}
                        smks = {}
                        for s in (0, 1):
                            d = st[s]
                            h = d["h"]
                            sslot = s
                            def mmS0(E, m0=m0, sslot=sslot, cc0=cc0, h=h, j=j):
                                return E.matmul(
                                    PS[:, sslot * 1024 + cc0: sslot * 1024 + QT],
                                    AB.sl(o_kT + m0 * 128, 128), qrhs(h, j, cc0),
                                    start=True, stop=True)
                            em.op("tensor", mmS0, waits=[("s", slot_last_exp[sslot])])
                            def mmS1(E, m1=m1, sslot=sslot, cc1=cc1, h=h, j=j):
                                return E.matmul(
                                    PS[:, sslot * 1024 + 512: sslot * 1024 + 512 + QT - cc1],
                                    AB.sl(o_kT + m1 * 128, 128), qrhs(h, j, cc1),
                                    start=True, stop=True)
                            smks[s] = em.op("tensor", mmS1, inc="t")
                        for s in (0, 1):
                            if g > 0:
                                emit_pv(s, g - 1)
                            if g % 4 == 0 and g > 0:
                                emit_ones(s, g // 4 - 1, glast=False)
                        for s in (0, 1):
                            smk = smks[s]
                            w_ = [("t", smk)]
                            ph = P_hist[(s, g % 2)]
                            if ph:
                                w_ += [("g", ph[2])]
                            def ex(E, s=s, g=g, cc0=cc0, cc1=cc1):
                                return E.activation(
                                    AB.sl(o_P + (2 * s + g % 2) * 1024 + cc0,
                                          512 + QT - cc1 - cc0),
                                    PS[:, s * 1024 + cc0: s * 1024 + 512 + QT - cc1],
                                    AF.Exp, scale=SCALE)
                            emk = em.op("scalar", ex, waits=w_, inc="s")
                            emks[s] = emk
                            slot_last_exp[s] = emk
                        first_of_group = (g % 4 == 0)
                        for s in (0, 1):
                            d = st[s]
                            vw = [("s", emks[s])]
                            if first_of_group:
                                po = d["ones_mark"].get(g // 4 - 1,
                                                        prev_last_ones[s])
                                if po:
                                    vw.append(("t", po))
                            for ii, m, cc in ((0, m0, cc0), (1, m1, cc1)):
                                if m >= M - NBAND:
                                    def bd(E, ii=ii, g=g, cc=cc, s=s):
                                        off = 512 if ii else cc
                                        pp = AB.sl(o_P + (2 * s + g % 2) * 1024 + off, 16)
                                        return E.tensor_tensor(pp, pp, sb_stair[:, :],
                                                               AL.mult)
                                    em.op("vector", bd, waits=vw)
                                    vw = []
                            def a0(E, g=g, cc=cc0, s=s, cp=first_of_group):
                                dst = AB.sl(o_A0 + s * 512 + cc, QT - cc)
                                srcp = AB.sl(o_P + (2 * s + g % 2) * 1024 + cc, QT - cc)
                                if cp:
                                    return E.tensor_copy(dst, srcp)
                                return E.tensor_tensor(dst, dst, srcp, AL.add)
                            vm = em.op("vector", a0, waits=vw, inc="v")
                            d["pair_v"][g] = vm
                        for s in (0, 1):
                            d = st[s]
                            gw = [("v", d["pair_v"][g])]
                            if first_of_group:
                                po = d["ones_mark"].get(g // 4 - 1,
                                                        prev_last_ones[s])
                                if po:
                                    gw.append(("t", po))
                            def a1(E, g=g, cc=cc1, s=s, cp=first_of_group):
                                dst = AB.sl(o_A1 + s * 512 + cc, QT - cc)
                                srcp = AB.sl(o_P + (2 * s + g % 2) * 1024 + 512, QT - cc)
                                if cp:
                                    return E.tensor_copy(dst, srcp)
                                return E.tensor_tensor(dst, dst, srcp, AL.add)
                            d["pair_g"][g] = em.op("gpsimd", a1, waits=gw, inc="g")

                    for g in range(NP):
                        emit_gstep(g)
                    lastpv = {}
                    for s in (0, 1):
                        lastpv[s] = emit_pv(s, NP - 1)
                        prev_last_ones[s] = emit_ones(s, M // 8 - 1, glast=True)
                    sm1 = {}
                    for s in (0, 1):
                        def drow(E, s=s, d=st[s]):
                            return E.copy(AB.sl(o_dr + s * 512, QT, 0, 1),
                                          PS[0:1, d["dbank"] * 512: d["dbank"] * 512 + QT])
                        sm1[s] = em.op("scalar", drow,
                                       waits=[("t", prev_last_ones[s])], inc="s")
                    tm1 = {}
                    for s in (0, 1):
                        def bcast(E, s=s, d=st[s]):
                            return E.matmul(bank(d["dbank"], QT), ones_row,
                                            AB.sl(o_dr + s * 512, QT, 0, 1),
                                            start=True, stop=True)
                        tm1[s] = em.op("tensor", bcast, waits=[("s", sm1[s])], inc="t")
                    sm2 = {}
                    for s in (0, 1):
                        def bcp(E, s=s, d=st[s]):
                            return E.copy(FA.sl(o_Bsb + s * 512, QT), bank(d["dbank"], QT))
                        sm2[s] = em.op("scalar", bcp, waits=[("t", tm1[s])], inc="s")
                        d_free[s] = sm2[s]
                    rcpm = {}
                    for s in (0, 1):
                        def rcp(E, s=s):
                            return E.reciprocal(FA.sl(o_Rsb + s * 512, QT),
                                                FA.sl(o_Bsb + s * 512, QT))
                        rcpm[s] = em.op("vector", rcp, waits=[("s", sm2[s])], inc="v")
                    for s in (0, 1):
                        def dvd(E, s=s, j=j, d=st[s]):
                            return E.tensor_tensor(
                                AB.sl(o_qct + d["h"] * OB + j * QT, QT),
                                bank(d["cbank"], QT),
                                FA.sl(o_Rsb + s * 512, QT), AL.mult)
                        ctx_free[s] = em.op("vector", dvd,
                                            waits=[("t", lastpv[s]), ("v", rcpm[s])],
                                            inc="v")
            barrier()
            FA.pop()
            AB.pop()

            em.phase = f"outln{b}"
            AB.push()
            o_s1 = AB.alloc(DC * OB)
            o_sq = AB.alloc(2 * 512)
            o_ow = AB.alloc(2 * DC * 128)
            o_mB = AB.alloc(2 * 512)
            o_rB = AB.alloc(2 * 512)
            o_tc = AB.alloc(2 * 512)
            o_mr = AB.alloc(2 * 512)
            o_rr = AB.alloc(2 * 512)
            FA.push()
            o_xr = FA.alloc(4 * 512)
            o_t1 = FA.alloc(2 * 512)
            o_rw = FA.alloc(2 * 1024)
            o_scr1 = FA.alloc(2 * 512)

            def s1sl(fc, tq):
                return AB.sl(o_s1 + (fc * NQT + tq) * QT, QT)

            ow_mark, xr_mark, s1_mark, sq_t = {}, {}, {}, {}
            msq_mark = {}
            stat_last = None
            for fc in range(DC):
                slot = fc % 2
                w_ = [ow_mark[fc - 2]] if fc >= 2 else []
                def dow(E, fc=fc, slot=slot):
                    return E.dma_start(
                        AB.sl(o_ow + slot * DC * 128, DC * 128).rearrange(
                            "p (c f) -> p c f", c=DC),
                        outwT[:, fc * 128:(fc + 1) * 128].rearrange(
                            "(c p) f -> p c f", p=128))
                dm = em.op("gpsimd", dow, waits=w_, inc="gd", amt=16)
                for tq in range(NQT):
                    i_ = fc * NQT + tq
                    xslot = i_ % 4
                    w_ = [xr_mark[i_ - 4]] if i_ >= 4 else []
                    def dxr(E, fc=fc, tq=tq, xslot=xslot, b=b):
                        return E.dma_start(
                            FA.sl(o_xr + xslot * 512, QT),
                            xTown[fc * 128:(fc + 1) * 128,
                                  b * OB + tq * QT:b * OB + (tq + 1) * QT])
                    xdm = em.op("sync", dxr, waits=w_, inc="sd", amt=16)
                    bnk = i_ % 2
                    for dc in range(DC):
                        w_ = ([("gd", dm)] + bank_w[bnk]) if dc == 0 else []
                        def mmo(E, dc=dc, tq=tq, slot=slot, bnk=bnk):
                            return E.matmul(
                                bank(bnk, QT),
                                AB.sl(o_ow + slot * DC * 128 + dc * 128, 128),
                                AB.sl(o_qct + dc * OB + tq * QT, QT),
                                start=(dc == 0), stop=(dc == DC - 1))
                        om = em.op("tensor", mmo, waits=w_,
                                   inc=("t" if dc == DC - 1 else None))
                    ow_mark[fc] = ("t", om)
                    tslot = i_ % 2
                    w_ = [("t", om), ("sd", xdm)]
                    if i_ >= 2:
                        w_.append(("s", s1_mark[i_ - 2]))
                    def ev1(E, bnk=bnk, xslot=xslot, tslot=tslot):
                        return E.tensor_tensor(
                            FA.sl(o_t1 + tslot * 512, QT), bank(bnk, QT),
                            FA.sl(o_xr + xslot * 512, QT), AL.add)
                    vm = em.op("vector", ev1, waits=w_, inc="v")
                    xr_mark[i_] = ("v", vm)
                    bank_w[bnk] = [("v", vm)]
                    def ev2(E, fc=fc, tq=tq, tslot=tslot):
                        return E.activation(
                            s1sl(fc, tq), FA.sl(o_t1 + tslot * 512, QT),
                            AF.Identity, bias=sb_outb[:, fc:fc + 1], scale=1.0)
                    s1_mark[i_] = em.op("scalar", ev2, waits=[("v", vm)], inc="s")
                if fc >= 1:
                    for tq in range(NQT):
                        w_ = [msq_mark[(fc - 2, tq)]] if (fc - 2, tq) in msq_mark else []
                        def sq_(E, fc=fc, tq=tq):
                            return E.activation(AB.sl(o_sq + tq * 512, QT),
                                                s1sl(fc - 1, tq), AF.Square, scale=1.0)
                        smq = em.op("scalar", sq_, waits=w_, inc="s")
                        stb, sqb = 2 + tq, 2 + NQT + tq
                        def mst(E, fc=fc, tq=tq, stb=stb):
                            return E.matmul(bank(stb, QT, rows=(0, 1)), ones_col,
                                            s1sl(fc - 1, tq),
                                            start=(fc == 1), stop=False)
                        em.op("tensor", mst,
                              waits=[("s", s1_mark[(fc - 1) * NQT + tq])] +
                                    (bank_w[stb] if fc == 1 else []))
                        def msq(E, fc=fc, tq=tq, sqb=sqb):
                            return E.matmul(bank(sqb, QT, rows=(0, 1)), ones_col,
                                            AB.sl(o_sq + tq * 512, QT),
                                            start=(fc == 1), stop=False)
                        msq_mark[(fc - 1, tq)] = ("t", em.op(
                            "tensor", msq,
                            waits=[("s", smq)] + (bank_w[sqb] if fc == 1 else []),
                            inc="t"))
            for tq in range(NQT):
                w_ = [msq_mark[(DC - 2, tq)]] if (DC - 2, tq) in msq_mark else []
                def sq_(E, tq=tq):
                    return E.activation(AB.sl(o_sq + tq * 512, QT),
                                        s1sl(DC - 1, tq), AF.Square, scale=1.0)
                smq = em.op("scalar", sq_, waits=w_, inc="s")
                stb, sqb = 2 + tq, 2 + NQT + tq
                def mst(E, tq=tq, stb=stb):
                    return E.matmul(bank(stb, QT, rows=(0, 1)), ones_col,
                                    s1sl(DC - 1, tq), start=(DC == 1), stop=True)
                em.op("tensor", mst, waits=[("s", s1_mark[(DC - 1) * NQT + tq])])
                def msq(E, tq=tq, sqb=sqb):
                    return E.matmul(bank(sqb, QT, rows=(0, 1)), ones_col,
                                    AB.sl(o_sq + tq * 512, QT), start=(DC == 1), stop=True)
                stat_last = em.op("tensor", msq, waits=[("s", smq)], inc="t")

            cpk1 = ln_rows(NQT, QT, o_rw, o_scr1, stat_last,
                           lambda t: 2 + t, lambda t: 2 + NQT + t, o_mr, o_rr)
            s1_bc = {}
            for tq in range(NQT):
                bb_ = 6 + (tq % 2)
                def bm1(E, tq=tq, bb_=bb_):
                    return E.matmul(bank(bb_, QT), ones_row,
                                    AB.sl(o_mr + tq * 512, QT, 0, 1),
                                    start=True, stop=True)
                tb1 = em.op("tensor", bm1, waits=[("v", cpk1[tq])] + bank_w[bb_], inc="t")
                def cb1(E, tq=tq, bb_=bb_):
                    return E.copy(AB.sl(o_mB + tq * 512, QT), bank(bb_, QT))
                sb1 = em.op("scalar", cb1, waits=[("t", tb1)], inc="s")
                def bm2(E, tq=tq, bb_=bb_):
                    return E.matmul(bank(bb_, QT), ones_row,
                                    AB.sl(o_rr + tq * 512, QT, 0, 1),
                                    start=True, stop=True)
                tb2 = em.op("tensor", bm2, waits=[("s", sb1)], inc="t")
                def cb2(E, tq=tq, bb_=bb_):
                    return E.copy(AB.sl(o_rB + tq * 512, QT), bank(bb_, QT))
                sb2 = em.op("scalar", cb2, waits=[("t", tb2)], inc="s")
                bank_w[bb_] = [("s", sb2)]
                s1_mark[("bc", tq)] = sb2
            pend = []
            for fc in range(DC):
                for tq in range(NQT):
                    tslot = (fc * NQT + tq) % 2
                    w_ = [("s", s1_mark[("bc", tq)])] if fc == 0 else []
                    def fsub(E, fc=fc, tq=tq, tslot=tslot):
                        return E.tensor_tensor(
                            AB.sl(o_tc + tslot * 512, QT), s1sl(fc, tq),
                            AB.sl(o_mB + tq * 512, QT), AL.subtract)
                    em.op("vector", fsub, waits=w_)
                    pend.append((fc, tq, tslot))
                    if len(pend) == 2:
                        for fc_, tq_, ts_ in pend:
                            def fmul(E, fc_=fc_, tq_=tq_, ts_=ts_, b=b):
                                return E.tensor_tensor(
                                    x1sl(fc_, b * OB + tq_ * QT, QT),
                                    AB.sl(o_tc + ts_ * 512, QT),
                                    AB.sl(o_rB + tq_ * 512, QT), AL.mult)
                            em.op("vector", fmul, inc="v")
                        pend = []
            for fc_, tq_, ts_ in pend:
                def fmul(E, fc_=fc_, tq_=tq_, ts_=ts_, b=b):
                    return E.tensor_tensor(
                        x1sl(fc_, b * OB + tq_ * QT, QT),
                        AB.sl(o_tc + ts_ * 512, QT),
                        AB.sl(o_rB + tq_ * 512, QT), AL.mult)
                em.op("vector", fmul, inc="v")
            FA.pop()
            AB.pop()
            AB.pop()
            barrier()

        em.phase = "router"
        FA.push()
        o_ssb = FA.alloc(2 * 512)
        o_gx = FA.alloc(4 * 64)
        gate_done = 0
        for tt_i in range(NTS):
            sslot = tt_i % 2
            for dc in range(DC):
                w_ = bank_w[0] if dc == 0 else []
                def mr(E, tt_i=tt_i, dc=dc):
                    return E.matmul(
                        bank(0, TSW, rows=(0, 8)), AB.sl(o_rtr + dc * NE, NE),
                        AB.sl(o_x1 + dc * TOWN + tt_i * TSW, TSW),
                        start=(dc == 0), stop=(dc == DC - 1))
                rm = em.op("tensor", mr, waits=w_, inc=("t" if dc == DC - 1 else None))
            def cps(E, sslot=sslot):
                return E.copy(FA.sl(o_ssb + sslot * 512, TSW, 0, 8),
                              bank(0, TSW, rows=(0, 8)))
            sms = em.op("scalar", cps, waits=[("t", rm)], inc="s")
            bank_w[0] = [("s", sms)]
            NG = TSW // 128
            trm = {}
            for gg in range(NG):
                tb = 1 + gg % 2
                def tr(E, gg=gg, sslot=sslot, tb=tb):
                    return E.transpose(bank(tb, 8),
                                       FA.sl(o_ssb + sslot * 512 + gg * 128, 128, 0, 8),
                                       sb_identf[0:8, 0:8])
                trm[gg] = em.op("tensor", tr, waits=[("s", sms)] + bank_w[tb], inc="t")
                bank_w[tb] = []

            def gx(gg, k):
                return FA.sl(o_gx + gg * 64 + k * 8, 8)

            def gmr(gg, k):
                return FA.sl(o_gx + gg * 64 + 48 + k, 1)

            for gg in range(NG):
                def f(E, gg=gg, tb=1 + gg % 2):
                    return E.tensor_copy(gx(gg, 0), bank(tb, 8))
                vmc = em.op("vector", f, waits=[("t", trm[gg])], inc="v")
                bank_w[1 + gg % 2] = [("v", vmc)]
            for gg in range(NG):
                em.op("vector", lambda E, gg=gg: E.reduce_max(gmr(gg, 0), gx(gg, 0), axis=AX.X))
            for gg in range(NG):
                em.op("vector", lambda E, gg=gg: E.tensor_scalar(
                    gx(gg, 1), gx(gg, 0), gmr(gg, 0), None, AL.is_equal))
            for gg in range(NG):
                em.op("vector", lambda E, gg=gg: E.tensor_scalar(
                    gx(gg, 1), gx(gg, 1), sb_cst[:, K_NEG30:K_NEG30 + 1], None, AL.mult))
            for gg in range(NG):
                em.op("vector", lambda E, gg=gg: E.tensor_tensor(
                    gx(gg, 1), gx(gg, 1), gx(gg, 0), AL.add))
            for gg in range(NG):
                em.op("vector", lambda E, gg=gg: E.reduce_max(gmr(gg, 1), gx(gg, 1), axis=AX.X))
            for gg in range(NG):
                em.op("vector", lambda E, gg=gg: E.tensor_scalar(
                    gx(gg, 2), gx(gg, 0), gmr(gg, 1), None, AL.is_ge))
            vmn = None
            for gg in range(NG):
                vmn = em.op("vector", lambda E, gg=gg: E.tensor_scalar(
                    gmr(gg, 2), gmr(gg, 0), -1.0, None, AL.mult), inc="v")
            for gg in range(NG):
                em.op("scalar", lambda E, gg=gg: E.activation(
                    gx(gg, 3), gx(gg, 0), AF.Exp, bias=gmr(gg, 2), scale=1.0),
                    waits=([("v", vmn)] if gg == 0 else []))
            sme = None
            for gg in range(NG):
                sme = em.op("scalar", lambda E, gg=gg: E.activation(
                    gmr(gg, 3), gmr(gg, 1), AF.Exp, bias=gmr(gg, 2), scale=1.0), inc="s")
            for gg in range(NG):
                em.op("vector", lambda E, gg=gg: E.tensor_scalar(
                    gmr(gg, 4), gmr(gg, 3), sb_cst[:, K_ONE:K_ONE + 1], None, AL.add),
                    waits=([("s", sme)] if gg == 0 else []))
            for gg in range(NG):
                em.op("vector", lambda E, gg=gg: E.reciprocal(gmr(gg, 5), gmr(gg, 4)))
            for gg in range(NG):
                em.op("vector", lambda E, gg=gg: E.tensor_scalar(
                    gx(gg, 4), gx(gg, 3), gmr(gg, 5), None, AL.mult))
            vmg = None
            for gg in range(NG):
                vmg = em.op("vector", lambda E, gg=gg: E.tensor_tensor(
                    gx(gg, 4), gx(gg, 4), gx(gg, 2), AL.mult), inc="v")
            for gg in range(NG):
                tb = 3 + gg % 2
                def trb(E, gg=gg, tb=tb):
                    return E.transpose(bank(tb, 128, rows=(0, 8)), gx(gg, 4),
                                      sb_identf[:, :])
                tmb = em.op("tensor", trb, waits=[("v", vmg)] + bank_w[tb], inc="t")
                def cpg(E, gg=gg, tt_i=tt_i, tb=tb):
                    return E.copy(AB.sl(o_gates + tt_i * TSW + gg * 128, 128, 0, 8),
                                  bank(tb, 128, rows=(0, 8)))
                gate_done = em.op("scalar", cpg, waits=[("t", tmb)], inc="s")
                bank_w[tb] = [("s", gate_done)]
        FA.pop()
        barrier()

        em.phase = "ffn"
        AB.push()
        o_y = AB.alloc(DC * TOWN)
        AB.push()
        o_h = AB.alloc(FFC * TOWN)
        o_wf = AB.alloc(2 * 2 * DC * 128)
        o_w2 = AB.alloc(2 * FFC * 128)
        o_gB = AB.alloc(TOWN)
        o_sil = AB.alloc(2 * 512)
        o_t2 = AB.alloc(2 * 512)

        wf_mark, w2_mark = {}, {}
        w2_last_prev = 0
        NTSG = (NTS + 1) // 2
        for e_i in range(NE):
            gB_mark = {}
            vprev = C["v"]
            for ts in range(NTS):
                gb_ = ts % 2
                def gbm(E, e_i=e_i, ts=ts, gb_=gb_):
                    return E.matmul(bank(gb_, TSW), AB.sl(o_esel + e_i * 128, 128, 0, 8),
                                    AB.sl(o_gates + ts * TSW, TSW, 0, 8),
                                    start=True, stop=True)
                tmg = em.op("tensor", gbm,
                            waits=[("s", gate_done)] + bank_w[gb_], inc="t")
                def gbc(E, ts=ts, gb_=gb_):
                    return E.copy(AB.sl(o_gB + ts * TSW, TSW), bank(gb_, TSW))
                smg = em.op("scalar", gbc,
                            waits=[("t", tmg), ("v", vprev)], inc="s")
                bank_w[gb_] = [("s", smg)]
                gB_mark[ts] = smg
            h_rd = {}
            first_of_expert = True
            for ffc in range(FFC):
                slot = ffc % 2
                w_ = []
                key = ("wf", e_i, ffc - 2)
                if key in wf_mark:
                    w_.append(wf_mark[key])
                elif e_i > 0:
                    key = ("wf", e_i - 1, ffc + FFC - 2)
                    if key in wf_mark:
                        w_.append(wf_mark[key])
                def dw1(E, e_i=e_i, ffc=ffc, slot=slot):
                    return E.dma_start(
                        AB.sl(o_wf + (slot * 2 + 0) * DC * 128, DC * 128).rearrange(
                            "p (c f) -> p c f", c=DC),
                        w1[e_i][:, ffc * 128:(ffc + 1) * 128].rearrange(
                            "(c p) f -> p c f", p=128))
                em.op("gpsimd", dw1, waits=w_, inc="gd", amt=16)
                def dw3(E, e_i=e_i, ffc=ffc, slot=slot):
                    return E.dma_start(
                        AB.sl(o_wf + (slot * 2 + 1) * DC * 128, DC * 128).rearrange(
                            "p (c f) -> p c f", c=DC),
                        w3[e_i][:, ffc * 128:(ffc + 1) * 128].rearrange(
                            "(c p) f -> p c f", p=128))
                dmw = em.op("gpsimd", dw3, inc="gd", amt=16)
                for tsg in range(NTSG):
                    par = (ffc * NTSG + tsg) % 2
                    tss = [tsg * 2 + k for k in range(2) if tsg * 2 + k < NTS]
                    bks = {}
                    for k, ts in enumerate(tss):
                        bks[(0, ts)] = 4 * par + k
                        bks[(1, ts)] = 4 * par + 2 + k
                    lastmm = None
                    hjobs = [(dc, w_i, k, ts)
                             for dc in range(DC) for w_i in (0, 1)
                             for k, ts in enumerate(tss)]
                    for dc, w_i, k, ts in hjobs:
                        bnk = bks[(w_i, ts)]
                        w_ = []
                        if dc == 0 and w_i == 0 and k == 0:
                            w_ = [("gd", dmw)]
                            for bq in sorted(set(bks.values())):
                                w_ += bank_w[bq]
                        def mmh(E, e_i=e_i, ffc=ffc, slot=slot, w_i=w_i,
                                ts=ts, dc=dc, bnk=bnk):
                            return E.matmul(
                                bank(bnk, TSW),
                                AB.sl(o_wf + (slot * 2 + w_i) * DC * 128 + dc * 128, 128),
                                AB.sl(o_x1 + dc * TOWN + ts * TSW, TSW),
                                start=(dc == 0), stop=(dc == DC - 1))
                        inc = "t" if (dc == DC - 1 and w_i == 1 and
                                      k == len(tss) - 1) else None
                        lastmm = em.op("tensor", mmh, waits=w_, inc=inc) or lastmm
                    wf_mark[("wf", e_i, ffc)] = ("t", lastmm)
                    silm = {}
                    prev_vmh = C["v"]
                    for k, ts in enumerate(tss):
                        w_ = [("t", lastmm), ("v", prev_vmh)]
                        def sil(E, ts=ts, k=k, bks=bks):
                            return E.activation(AB.sl(o_sil + k * 512, TSW),
                                                bank(bks[(0, ts)], TSW), AF.Silu,
                                                scale=1.0)
                        silm[ts] = em.op("scalar", sil, waits=w_, inc="s")
                    for k, ts in enumerate(tss):
                        w_ = [("s", silm[ts])]
                        if first_of_expert and k == 0:
                            w_.append(("s", gB_mark[tss[-1]]))
                            if w2_last_prev:
                                w_.append(("t", w2_last_prev))
                        def mu1(E, ts=ts, k=k, bks=bks):
                            return E.tensor_tensor(
                                AB.sl(o_t2 + k * 512, TSW), AB.sl(o_sil + k * 512, TSW),
                                bank(bks[(1, ts)], TSW), AL.mult)
                        em.op("vector", mu1, waits=w_)
                    first_of_expert = False
                    vmh = None
                    for k, ts in enumerate(tss):
                        def mu2(E, ts=ts, k=k, ffc=ffc):
                            return E.tensor_tensor(
                                AB.sl(o_h + ffc * TOWN + ts * TSW, TSW),
                                AB.sl(o_t2 + k * 512, TSW),
                                AB.sl(o_gB + ts * TSW, TSW), AL.mult)
                        vmh = em.op("vector", mu2, inc="v")
                        h_rd[(ffc, ts)] = vmh
                    for bq in sorted(set(bks.values())):
                        bank_w[bq] = [("v", vmh)]
            for fc in range(DC):
                slot = fc % 2
                w_ = [w2_mark[("w2", e_i, fc - 2)]] if ("w2", e_i, fc - 2) in w2_mark else []
                def dw2(E, e_i=e_i, fc=fc, slot=slot):
                    return E.dma_start(
                        AB.sl(o_w2 + slot * FFC * 128, FFC * 128).rearrange(
                            "p (c f) -> p c f", c=FFC),
                        w2[e_i][:, fc * 128:(fc + 1) * 128].rearrange(
                            "(c p) f -> p c f", p=128))
                dm2 = em.op("gpsimd", dw2, waits=w_, inc="gd", amt=16)
                for ts in range(NTS):
                    bnk = (fc * NTS + ts) % 4
                    for ffc in range(FFC):
                        w_ = []
                        if ffc == 0:
                            w_ = [("gd", dm2)] + bank_w[bnk]
                            w_ += [("v", h_rd[(fq, ts)]) for fq in range(FFC)]
                        def mmy(E, fc=fc, ts=ts, ffc=ffc, slot=slot, bnk=bnk):
                            return E.matmul(
                                bank(bnk, TSW),
                                AB.sl(o_w2 + slot * FFC * 128 + ffc * 128, 128),
                                AB.sl(o_h + ffc * TOWN + ts * TSW, TSW),
                                start=(ffc == 0), stop=(ffc == FFC - 1))
                        ym = em.op("tensor", mmy, waits=w_,
                                   inc=("t" if ffc == FFC - 1 else None))
                    w2_mark[("w2", e_i, fc)] = ("t", ym)
                    w2_last_prev = ym
                    def yac(E, e_i=e_i, fc=fc, ts=ts, bnk=bnk):
                        ysl = AB.sl(o_y + fc * TOWN + ts * TSW, TSW)
                        if e_i == 0:
                            return E.tensor_copy(ysl, bank(bnk, TSW))
                        return E.tensor_tensor(ysl, ysl, bank(bnk, TSW), AL.add)
                    vmy = em.op("vector", yac, waits=[("t", ym)], inc="v")
                    bank_w[bnk] = [("v", vmy)]
        barrier()
        AB.pop()

        em.phase = "ln2"
        s2m = None
        for fc in range(DC):
            def s2a(E, fc=fc):
                return E.tensor_tensor(AB.sl(o_y + fc * TOWN, TOWN),
                                       AB.sl(o_y + fc * TOWN, TOWN),
                                       AB.sl(o_x1 + fc * TOWN, TOWN), AL.add)
            s2m = em.op("vector", s2a, inc="v")
        AB.push()
        o_sq2 = AB.alloc(NTS * 512)
        o_mB2 = AB.alloc(2 * 512)
        o_rB2 = AB.alloc(2 * 512)
        o_tc2 = AB.alloc(2 * 512)
        o_mr2 = AB.alloc(NTS * 512)
        o_rr2 = AB.alloc(NTS * 512)
        FA.push()
        o_rw2 = FA.alloc(NTS * 1024)
        o_ot = FA.alloc(2 * 512)
        o_scr2 = FA.alloc(2 * 512)
        msq2_mark = {}
        stat2_last = None
        for fc in range(DC):
            for ts in range(NTS):
                w_ = [("v", s2m)] if fc == 0 else []
                if (fc - 1, ts) in msq2_mark:
                    w_.append(msq2_mark[(fc - 1, ts)])
                def sq2(E, fc=fc, ts=ts):
                    return E.activation(AB.sl(o_sq2 + ts * 512, TSW),
                                        AB.sl(o_y + fc * TOWN + ts * TSW, TSW),
                                        AF.Square, scale=1.0)
                smq = em.op("scalar", sq2, waits=w_, inc="s")
                stb, sqb = ts % 4, 4 + ts % 4
                def mst2(E, fc=fc, ts=ts, stb=stb):
                    return E.matmul(bank(stb, TSW, rows=(0, 1)), ones_col,
                                    AB.sl(o_y + fc * TOWN + ts * TSW, TSW),
                                    start=(fc == 0), stop=(fc == DC - 1))
                em.op("tensor", mst2,
                      waits=([("v", s2m)] + bank_w[stb]) if fc == 0 else [])
                def msq2(E, fc=fc, ts=ts, sqb=sqb):
                    return E.matmul(bank(sqb, TSW, rows=(0, 1)), ones_col,
                                    AB.sl(o_sq2 + ts * 512, TSW),
                                    start=(fc == 0), stop=(fc == DC - 1))
                tms = em.op("tensor", msq2,
                            waits=[("s", smq)] + (bank_w[sqb] if fc == 0 else []),
                            inc="t")
                msq2_mark[(fc, ts)] = ("t", tms)
                stat2_last = tms
        cpk2 = ln_rows(NTS, TSW, o_rw2, o_scr2, stat2_last,
                       lambda t: t % 4, lambda t: 4 + t % 4, o_mr2, o_rr2)
        rowv = cpk2
        apply_v = {}
        ot_mark = {}
        for ts in range(NTS):
            rs = ts % 2
            bb1, bb2 = rs, 2 + rs
            w_ = [("v", rowv[ts])] + bank_w[bb1]
            if ts >= 2:
                w_.append(("v", apply_v[ts - 2]))
            def bm1(E, ts=ts, rs=rs, bb1=bb1):
                return E.matmul(bank(bb1, TSW), ones_row,
                                AB.sl(o_mr2 + ts * 512, TSW, 0, 1), start=True, stop=True)
            tb1 = em.op("tensor", bm1, waits=w_, inc="t")
            def cb1(E, ts=ts, rs=rs, bb1=bb1):
                return E.copy(AB.sl(o_mB2 + rs * 512, TSW), bank(bb1, TSW))
            sb1 = em.op("scalar", cb1, waits=[("t", tb1)], inc="s")
            bank_w[bb1] = [("s", sb1)]
            def bm2(E, ts=ts, rs=rs, bb2=bb2):
                return E.matmul(bank(bb2, TSW), ones_row,
                                AB.sl(o_rr2 + ts * 512, TSW, 0, 1), start=True, stop=True)
            tb2 = em.op("tensor", bm2, waits=[("v", rowv[ts])] + bank_w[bb2], inc="t")
            def cb2(E, ts=ts, rs=rs, bb2=bb2):
                return E.copy(AB.sl(o_rB2 + rs * 512, TSW), bank(bb2, TSW))
            sb2 = em.op("scalar", cb2, waits=[("t", tb2)], inc="s")
            bank_w[bb2] = [("s", sb2)]
            pend = []
            lastv = None
            for fc in range(DC):
                tsl = fc % 2
                w_ = [("s", sb2)] if fc == 0 else []
                def fs(E, fc=fc, ts=ts, tsl=tsl, rs=rs):
                    return E.tensor_tensor(
                        AB.sl(o_tc2 + tsl * 512, TSW),
                        AB.sl(o_y + fc * TOWN + ts * TSW, TSW),
                        AB.sl(o_mB2 + rs * 512, TSW), AL.subtract)
                em.op("vector", fs, waits=w_)
                pend.append((fc, tsl))
                if len(pend) == 2:
                    for fc_, tsl_ in pend:
                        osl = fc_ % 2
                        w_ = []
                        k1, k2 = (fc_ - 2, ts), (fc_ - 2 + DC, ts - 1)
                        if k1 in ot_mark:
                            w_.append(ot_mark[k1])
                        elif k2 in ot_mark:
                            w_.append(ot_mark[k2])
                        def fm(E, fc_=fc_, ts=ts, tsl_=tsl_, osl=osl, rs=rs):
                            return E.tensor_tensor(
                                FA.sl(o_ot + osl * 512, TSW),
                                AB.sl(o_tc2 + tsl_ * 512, TSW),
                                AB.sl(o_rB2 + rs * 512, TSW), AL.mult)
                        lastv = em.op("vector", fm, waits=w_, inc="v")
                        def dout(E, fc_=fc_, ts=ts, osl=osl):
                            return E.dma_start(
                                out[fc_ * 128:(fc_ + 1) * 128, ts * TSW:(ts + 1) * TSW],
                                FA.sl(o_ot + osl * 512, TSW))
                        dmo = em.op("sync", dout, waits=[("v", lastv)], inc="sd", amt=16)
                        ot_mark[(fc_, ts)] = ("sd", dmo)
                    pend = []
            apply_v[ts] = lastv
        FA.pop()
        AB.pop()
        AB.pop()
        em.op("sync", None, waits=[("sd", C["sd"])])

        em.check()

        def runner(eng):
            def f(E):
                cur, sid = None, None
                for w, fn, inc, amt, ph in em.prog[eng]:
                    if ph != cur:
                        if cur is not None:
                            nc.leave_named_scope(cur, sid, False)
                        sid = nc.enter_named_scope(ph, False)[0]
                        cur = ph
                    for sn, val in w:
                        E.wait_ge(sems[sn], val)
                    inst = fn(E) if fn is not None else None
                    if inc is not None:
                        if inst is None:
                            inst = E.nop()
                        inst.then_inc(sems[inc], amt)
                if cur is not None:
                    nc.leave_named_scope(cur, sid, False)
            return f
        for e in ENG:
            getattr(blk, e)(runner(e))
    return nc, dv


def _prep_inputs(cfg, inputs):
    dv = _derive(cfg)
    B, D, E = cfg["B"], cfg["D"], cfg["E"]
    DC, OB = dv["DC"], dv["OB"]
    x = np.asarray(inputs["L0_attn_x"], np.float32)
    xT = np.ascontiguousarray(x.transpose(0, 2, 1))
    wqkvT = np.ascontiguousarray(np.asarray(inputs["L0_attn_Wqkv_w"], np.float32).T)
    wqkvb = np.ascontiguousarray(
        np.asarray(inputs["L0_attn_Wqkv_b"], np.float32).reshape(-1, 128).T)
    outwT = np.ascontiguousarray(np.asarray(inputs["L0_attn_out_w"], np.float32).T)
    outb = np.ascontiguousarray(
        np.asarray(inputs["L0_attn_out_b"], np.float32).reshape(DC, 128).T)
    routerT = np.ascontiguousarray(np.asarray(inputs["L0_ffn_router_w"], np.float32).T)
    w1 = np.ascontiguousarray(np.asarray(inputs["L0_ffn_w1"], np.float32))
    w3 = np.ascontiguousarray(np.asarray(inputs["L0_ffn_w3"], np.float32))
    w2 = np.ascontiguousarray(np.asarray(inputs["L0_ffn_w2"], np.float32))
    identf = np.eye(128, dtype=np.float32)
    onesbf = np.ones((128, 128), dtype=np.float32)
    esel = np.zeros((8, E * 128), dtype=np.float32)
    for e_i in range(E):
        esel[e_i, e_i * 128:(e_i + 1) * 128] = 1.0
    cst = np.zeros((128, 8), dtype=np.float32)
    cst[:, 0] = 1.0
    cst[:, 1] = 1e-5
    cst[:, 2] = 1.0 / D
    cst[:, 3] = -1e30
    maps = []
    for c in range(NCORES):
        xTown = np.ascontiguousarray(
            np.concatenate([xT[b][:, c::NCORES] for b in range(B)], axis=1))
        r = np.arange(128).reshape(128, 1)
        ii = np.arange(16).reshape(1, 16)
        stair = (r <= 8 * ii + c).astype(np.float32)
        maps.append({
            "xT": xT, "xTown": xTown, "wqkvT": wqkvT, "wqkvb": wqkvb,
            "outwT": outwT, "outb": outb, "routerT": routerT,
            "w1": w1, "w3": w3, "w2": w2, "identf": identf,
            "onesbf": onesbf, "stairbf": stair, "eselbf": esel, "cstf": cst,
        })
    return dv, maps


def kernel(**inputs):
    cfg = dict(CFG)
    dv, maps = _prep_inputs(cfg, inputs)
    nc, _ = build(cfg)
    res = run_bass_kernel_spmd(nc, maps, core_ids=list(range(NCORES)))
    B, S, D = cfg["B"], cfg["S"], cfg["D"]
    OB = dv["OB"]
    outx = np.empty((B, S, D), dtype=np.float32)
    for c in range(NCORES):
        oc = np.asarray(res.results[c]["out"])
        for b in range(B):
            outx[b, c::NCORES, :] = oc[:, b * OB:(b + 1) * OB].T
    return outx


# version 06 (final)
# speedup vs baseline: 1.2570x; 1.2570x over previous
import numpy as np
import ml_dtypes
import concourse.bass as bass
import concourse.mybir as mybir
from concourse.bass_utils import run_bass_kernel_spmd

F32 = mybir.dt.float32
BF16 = mybir.dt.bfloat16
AF = mybir.ActivationFunctionType
AL = mybir.AluOpType
AX = mybir.AxisListType
BFNP = ml_dtypes.bfloat16

NCORES = 8
ENG = ("tensor", "scalar", "vector", "gpsimd", "sync")

CFG = dict(B=2, S=8192, D=2048, E=8, FF=512)
USE_SCOPES = False
DEBUG_DUMP = False
MAXSYNC = False
MAXSYNC_ENG = None
MAXSYNC_PH = None


def _derive(cfg):
    d = dict(cfg)
    B, S, D, E, FF = d["B"], d["S"], d["D"], d["E"], d["FF"]
    d["H"] = D // 128
    d["DC"] = D // 128
    d["FFC"] = FF // 128
    d["OB"] = S // NCORES
    d["QT"] = min(512, d["OB"])
    d["NQT"] = d["OB"] // d["QT"]
    d["TOWN"] = B * d["OB"]
    d["TT"] = d["OB"] // 2
    d["NTT"] = S // d["TT"]
    d["TSW"] = min(512, d["TOWN"])
    d["NTS"] = d["TOWN"] // d["TSW"]
    d["NBAND"] = d["QT"] // 16
    assert d["OB"] == d["QT"] * d["NQT"] and d["TT"] % 128 == 0
    return d


class Arena:
    def __init__(self, tile, cols):
        self.tile, self.cols, self.off, self.stack = tile, cols, 0, []
        self.hwm = 0

    def alloc(self, n):
        assert self.off + n <= self.cols, f"arena overflow {self.off}+{n}>{self.cols}"
        off = self.off
        self.off += n
        self.hwm = max(self.hwm, self.off)
        return off

    def push(self):
        self.stack.append(self.off)

    def pop(self):
        self.off = self.stack.pop()

    def sl(self, off, n, p0=0, p1=128):
        return self.tile[p0:p1, off:off + n]


class Em:
    def __init__(self):
        self.prog = {e: [] for e in ENG}
        self.cnt = {n: 0 for n in ("t", "s", "v", "g", "gd", "sd", "bar")}
        self.phase = "init"

    def op(self, eng, fn, waits=(), inc=None, amt=1):
        if MAXSYNC and (MAXSYNC_ENG is None or eng in MAXSYNC_ENG) and (
                MAXSYNC_PH is None or any(self.phase.startswith(p) for p in MAXSYNC_PH)):
            waits = list(waits) + [(n, self.cnt[n]) for n in
                                   ("t", "s", "v", "g", "gd", "sd")]
        w = tuple((sn, val) for (sn, val) in waits if val)
        self.prog[eng].append((w, fn, inc, amt, self.phase))
        if inc is not None:
            self.cnt[inc] += amt
            return self.cnt[inc]
        return None

    def check(self):
        tot = dict(self.cnt)
        sem = {n: 0 for n in tot}
        ptr = {e: 0 for e in ENG}
        for e in ENG:
            for w, _, _, _, _ in self.prog[e]:
                for sn, val in w:
                    assert val <= tot[sn], f"{e}: wait {sn}>={val} > total {tot[sn]}"
        progressed = True
        while progressed:
            progressed = False
            for e in ENG:
                while ptr[e] < len(self.prog[e]):
                    w, _, inc, amt, _ = self.prog[e][ptr[e]]
                    if all(sem[sn] >= val for sn, val in w):
                        if inc is not None:
                            sem[inc] += amt
                        ptr[e] += 1
                        progressed = True
                    else:
                        break
        stuck = {e: (ptr[e], len(self.prog[e])) for e in ENG if ptr[e] < len(self.prog[e])}
        assert not stuck, f"schedule deadlock at {stuck}"


def build(cfg):
    dv = _derive(cfg)
    B, S, D, NE, FF = dv["B"], dv["S"], dv["D"], dv["E"], dv["FF"]
    H, DC, FFC = dv["H"], dv["DC"], dv["FFC"]
    OB, QT, NQT, TOWN = dv["OB"], dv["QT"], dv["NQT"], dv["TOWN"]
    TT, NTT, TSW, NTS, NBAND = dv["TT"], dv["NTT"], dv["TSW"], dv["NTS"], dv["NBAND"]
    KVF = 256
    NQKV = D + KVF
    SCALE = float(np.float32(1.0 / np.sqrt(128.0)))

    nc = bass.Bass("TRN2", target_bir_lowering=False)

    xT = nc.dram_tensor("xT", [B, D, S], F32, kind="ExternalInput").ap()
    xTown = nc.dram_tensor("xTown", [D, TOWN], F32, kind="ExternalInput").ap()
    wqkvT = nc.dram_tensor("wqkvT", [D, NQKV], F32, kind="ExternalInput").ap()
    wqkvb = nc.dram_tensor("wqkvb", [128, NQKV // 128], F32, kind="ExternalInput").ap()
    outwT = nc.dram_tensor("outwT", [D, D], F32, kind="ExternalInput").ap()
    outb = nc.dram_tensor("outb", [128, DC], F32, kind="ExternalInput").ap()
    routerT = nc.dram_tensor("routerT", [D, NE], F32, kind="ExternalInput").ap()
    w1 = nc.dram_tensor("w1", [NE, D, FF], F32, kind="ExternalInput").ap()
    w3 = nc.dram_tensor("w3", [NE, D, FF], F32, kind="ExternalInput").ap()
    w2 = nc.dram_tensor("w2", [NE, FF, D], F32, kind="ExternalInput").ap()
    identf = nc.dram_tensor("identf", [128, 128], F32, kind="ExternalInput").ap()
    onesbf = nc.dram_tensor("onesbf", [128, 128], F32, kind="ExternalInput").ap()
    stairbf = nc.dram_tensor("stairbf", [128, 16], F32, kind="ExternalInput").ap()
    eselbf = nc.dram_tensor("eselbf", [8, NE * 128], F32, kind="ExternalInput").ap()
    cstf = nc.dram_tensor("cstf", [128, 8], F32, kind="ExternalInput").ap()
    out = nc.dram_tensor("out", [D, TOWN], F32, kind="ExternalOutput").ap()
    dbg_x1 = nc.dram_tensor("dbg_x1", [128, DC * TOWN], F32, kind="ExternalOutput").ap()
    dbg_ctx = nc.dram_tensor("dbg_ctx", [128, H * OB], F32, kind="ExternalOutput").ap()
    dbg_g = nc.dram_tensor("dbg_g", [8, TOWN], F32, kind="ExternalOutput").ap()

    em = Em()
    C = em.cnt

    import contextlib
    with contextlib.ExitStack() as _es:
        _e = _es.enter_context
        ABt = _e(nc.sbuf_tensor("AB", [128, 91500], BF16))
        AFt = _e(nc.sbuf_tensor("AF", [128, 6600], F32))
        sb_identf = _e(nc.sbuf_tensor("sb_identf", [128, 128], F32))
        sb_ones = _e(nc.sbuf_tensor("sb_ones", [128, 128], BF16))
        sb_stair = _e(nc.sbuf_tensor("sb_stair", [128, 16], BF16))
        sb_qkvb = _e(nc.sbuf_tensor("sb_qkvb", [128, NQKV // 128], F32))
        sb_outb = _e(nc.sbuf_tensor("sb_outb", [128, DC], F32))
        sb_cst = _e(nc.sbuf_tensor("sb_cst", [128, 8], F32))
        sem_t = _e(nc.semaphore("sem_t"))
        sem_s = _e(nc.semaphore("sem_s"))
        sem_v = _e(nc.semaphore("sem_v"))
        sem_g = _e(nc.semaphore("sem_g"))
        sem_gd = _e(nc.semaphore("sem_gd"))
        sem_sd = _e(nc.semaphore("sem_sd"))
        sem_bar = _e(nc.semaphore("sem_bar"))
        PS = _e(nc.psum_tensor("PS", [128, 4096], F32))
        blk = _e(nc.Block())
        sems = {"t": sem_t, "s": sem_s, "v": sem_v, "g": sem_g,
                "gd": sem_gd, "sd": sem_sd, "bar": sem_bar}
        AB = Arena(ABt, 91500)
        FA = Arena(AFt, 6600)
        bank_w = {i: [] for i in range(8)}

        def bank(i, w=512, rows=(0, 128)):
            return PS[rows[0]:rows[1], i * 512:i * 512 + w]

        def barrier():
            tgt = C["bar"] + 5
            for e in ENG:
                em.op(e, lambda E: E.drain(), inc="bar")
            real = [(n, C[n]) for n in ("t", "s", "v", "g", "gd", "sd") if C[n]]
            for e in ENG:
                em.op(e, None, waits=[("bar", tgt)] + real)
            for i in range(8):
                bank_w[i] = []

        K_ONE, K_EPS, K_INVD, K_NEG30 = 0, 1, 2, 3

        em.phase = "consts"
        em.op("sync", lambda E: E.dma_start(sb_identf[:, :], identf[:, :]), inc="sd", amt=16)
        em.op("gpsimd", lambda E: E.dma_start(sb_ones[:, :], onesbf[:, :]), inc="gd", amt=16)
        em.op("gpsimd", lambda E: E.dma_start(sb_stair[:, :], stairbf[:, :]), inc="gd", amt=16)
        em.op("sync", lambda E: E.dma_start(sb_qkvb[:, :], wqkvb[:, :]), inc="sd", amt=16)
        em.op("sync", lambda E: E.dma_start(sb_outb[:, :], outb[:, :]), inc="sd", amt=16)
        cm = em.op("sync", lambda E: E.dma_start(sb_cst[:, :], cstf[:, :]), inc="sd", amt=16)
        o_x1 = AB.alloc(DC * TOWN)
        o_gates = AB.alloc(TOWN)
        o_rtr = AB.alloc(DC * NE)
        o_esel = AB.alloc(NE * 128)
        em.op("gpsimd", lambda E: E.dma_start(
            AB.sl(o_rtr, DC * NE).rearrange("p (c e) -> p c e", c=DC),
            routerT.rearrange("(c p) e -> p c e", p=128)), inc="gd", amt=16)
        gm0 = em.op("gpsimd", lambda E: E.dma_start(
            AB.sl(o_esel, NE * 128, 0, 8), eselbf[:, :]), inc="gd", amt=16)
        for e in ENG:
            em.op(e, None, waits=[("sd", cm), ("gd", gm0)])

        ones_col = sb_ones[:, 0:1]
        ones_row = sb_ones[0:1, :]

        def x1sl(c, o0, n):
            return AB.sl(o_x1 + c * TOWN + o0, n)

        def ln_rows(n_t, W, o_rows, o_scr, stat_wait, stat_bank, sq_bank,
                    o_mrow, o_rrow):
            dmk, fmk = {}, {}
            for t in range(n_t):
                def r0(E, t=t):
                    return E.tensor_scalar(
                        FA.sl(o_rows + t * 1024, W, 0, 1),
                        bank(stat_bank(t), W, rows=(0, 1)),
                        sb_cst[0:1, K_INVD:K_INVD + 1], None, AL.mult)
                em.op("vector", r0, waits=([("t", stat_wait)] if t == 0 else []))
            vr1 = None
            for t in range(n_t):
                def r1(E, t=t):
                    return E.tensor_scalar(
                        FA.sl(o_rows + t * 1024 + 512, W, 0, 1),
                        bank(sq_bank(t), W, rows=(0, 1)),
                        sb_cst[0:1, K_INVD:K_INVD + 1], None, AL.mult)
                vr1 = em.op("vector", r1, inc="v")
            for t in range(n_t):
                w_ = [("v", vr1)] if t == 0 else []
                if t - 2 in dmk:
                    w_.append(("v", dmk[t - 2]))
                def rq(E, t=t):
                    return E.activation(FA.sl(o_scr + (t % 2) * 512, W, 0, 1),
                                        FA.sl(o_rows + t * 1024, W, 0, 1),
                                        AF.Square, scale=1.0)
                em.op("scalar", rq, waits=w_, inc="s")
                def rd_(E, t=t):
                    r = FA.sl(o_rows + t * 1024 + 512, W, 0, 1)
                    return E.tensor_tensor(r, r, FA.sl(o_scr + (t % 2) * 512, W, 0, 1),
                                           AL.subtract)
                dmk[t] = em.op("vector", rd_, waits=[("s", C["s"])], inc="v")
            for t in range(n_t):
                w_ = [("v", dmk[t])]
                if t - 2 in fmk:
                    w_.append(("v", fmk[t - 2]))
                def rs_(E, t=t):
                    return E.activation(FA.sl(o_scr + (t % 2) * 512, W, 0, 1),
                                        FA.sl(o_rows + t * 1024 + 512, W, 0, 1),
                                        AF.Sqrt, bias=sb_cst[0:1, K_EPS:K_EPS + 1],
                                        scale=1.0)
                em.op("scalar", rs_, waits=w_, inc="s")
                def rr_(E, t=t):
                    return E.reciprocal(FA.sl(o_rows + t * 1024 + 512, W, 0, 1),
                                        FA.sl(o_scr + (t % 2) * 512, W, 0, 1))
                fmk[t] = em.op("vector", rr_, waits=[("s", C["s"])], inc="v")
            cpk = {}
            for t in range(n_t):
                def c1(E, t=t):
                    return E.tensor_copy(AB.sl(o_mrow + t * 512, W, 0, 1),
                                         FA.sl(o_rows + t * 1024, W, 0, 1))
                em.op("vector", c1)
                def c2(E, t=t):
                    return E.tensor_copy(AB.sl(o_rrow + t * 512, W, 0, 1),
                                         FA.sl(o_rows + t * 1024 + 512, W, 0, 1))
                cpk[t] = em.op("vector", c2, inc="v")
            return cpk



        for b in range(B):
            AB.push()
            o_qct = AB.alloc(H * OB)
            AB.push()
            o_kT = AB.alloc(S)
            o_vKM = AB.alloc(S)
            o_P = AB.alloc(4 * 1024)
            o_A0 = AB.alloc(2 * 512)
            o_A1 = AB.alloc(2 * 512)
            o_dr = AB.alloc(2 * 512)
            o_vsp = AB.alloc(16)
            FA.push()
            o_Bsb = FA.alloc(2 * 512)
            o_Rsb = FA.alloc(2 * 512)
            o_vtf = FA.alloc(2 * TT)

            def qrhs(h, j, c0):
                return AB.sl(o_qct + h * OB + j * QT + c0, QT - c0)

            em.phase = f"kvq{b}"
            AB.push()
            o_Wkv = AB.alloc(DC * KVF)
            o_wq = AB.alloc(2 * DC * 128)
            wkvm = em.op("gpsimd", lambda E: E.dma_start(
                AB.sl(o_Wkv, DC * KVF).rearrange("p (c f) -> p c f", c=DC),
                wqkvT[:, D:D + KVF].rearrange("(c p) f -> p c f", p=128)),
                inc="gd", amt=16)

            xt_mark = {}
            vt_tmark = {}
            NTR = TT // 128
            for tt_i in range(NTT):
                slot = tt_i % 2
                waits = [("gd", wkvm)] + ([xt_mark[tt_i - 2]] if tt_i >= 2 else [])
                def dxt(E, tt_i=tt_i, slot=slot, b=b):
                    d3 = ABt[:, o_x1:o_x1 + DC * TOWN].rearrange(
                        "p (c t) -> p c t", c=DC)[:, :, b * OB + slot * TT:
                                                  b * OB + slot * TT + TT]
                    s3 = xT[b][:, tt_i * TT:(tt_i + 1) * TT].rearrange(
                        "(c p) t -> p c t", p=128)
                    return E.dma_start(d3, s3)
                dm = em.op("gpsimd", dxt, waits=waits, inc="gd", amt=16)
                kb, vb = tt_i % 2, 2 + tt_i % 2
                mmk = {}
                for fi, bnk in ((0, kb), (1, vb)):
                    for dc in range(DC):
                        w_ = ([("gd", dm)] + bank_w[bnk]) if dc == 0 else []
                        def mm(E, fi=fi, bnk=bnk, dc=dc, slot=slot, b=b):
                            return E.matmul(
                                bank(bnk, TT), AB.sl(o_Wkv + dc * KVF + fi * 128, 128),
                                x1sl(dc, b * OB + slot * TT, TT),
                                start=(dc == 0), stop=(dc == DC - 1))
                        mk = em.op("tensor", mm, waits=w_,
                                   inc=("t" if dc == DC - 1 else None))
                    mmk[fi] = mk
                xt_mark[tt_i] = ("t", mmk[1])
                def evK(E, tt_i=tt_i, kb=kb):
                    return E.activation(AB.sl(o_kT + tt_i * TT, TT), bank(kb, TT),
                                        AF.Identity, bias=sb_qkvb[:, H:H + 1], scale=1.0)
                sm1 = em.op("scalar", evK, waits=[("t", mmk[0])], inc="s")
                bank_w[kb] = [("s", sm1)]
                wv = [("t", mmk[1])] + ([vt_tmark[tt_i - 2]] if tt_i >= 2 else [])
                def evV(E, tt_i=tt_i, vb=vb, slot=slot):
                    return E.activation(FA.sl(o_vtf + slot * TT, TT), bank(vb, TT),
                                        AF.Identity, bias=sb_qkvb[:, H + 1:H + 2], scale=1.0)
                sm2 = em.op("scalar", evV, waits=wv, inc="s")
                bank_w[vb] = [("s", sm2)]
                for q in range(NTR):
                    tb = 4 + (tt_i * NTR + q) % 2
                    def tr(E, q=q, tb=tb, slot=slot):
                        return E.transpose(bank(tb, 128),
                                           FA.sl(o_vtf + slot * TT + q * 128, 128),
                                           sb_identf[:, :])
                    tm = em.op("tensor", tr, waits=[("s", sm2)] + bank_w[tb], inc="t")
                    vt_tmark[tt_i] = ("t", tm)
                    def cp(E, tt_i=tt_i, q=q, tb=tb):
                        return E.tensor_copy(
                            AB.sl(o_vKM + (tt_i * NTR + q) * 128, 128), bank(tb, 128))
                    vm = em.op("vector", cp, waits=[("t", tm)], inc="v")
                    bank_w[tb] = [("v", vm)]

            qx_dma = {}
            kv_all = ("t", C["t"])
            for tq in range(NQT):
                def dqx(E, tq=tq, b=b):
                    d3 = ABt[:, o_x1:o_x1 + DC * TOWN].rearrange(
                        "p (c t) -> p c t", c=DC)[:, :, b * OB + tq * QT:
                                                  b * OB + (tq + 1) * QT]
                    s3 = xTown[:, b * OB + tq * QT:b * OB + (tq + 1) * QT].rearrange(
                        "(c p) t -> p c t", p=128)
                    return E.dma_start(d3, s3)
                qx_dma[tq] = em.op("gpsimd", dqx, waits=[kv_all], inc="gd", amt=16)
            wq_mark = {}
            for fc in range(H):
                slot = fc % 2
                w_ = [wq_mark[fc - 2]] if fc >= 2 else []
                def dwq(E, fc=fc, slot=slot):
                    return E.dma_start(
                        AB.sl(o_wq + slot * DC * 128, DC * 128).rearrange(
                            "p (c f) -> p c f", c=DC),
                        wqkvT[:, fc * 128:(fc + 1) * 128].rearrange(
                            "(c p) f -> p c f", p=128))
                dm = em.op("gpsimd", dwq, waits=w_, inc="gd", amt=16)
                for tq in range(NQT):
                    bnk = (fc * NQT + tq) % 4
                    for dc in range(DC):
                        w_ = ([("gd", dm), ("gd", qx_dma[tq])] + bank_w[bnk]) if dc == 0 else []
                        def mmq(E, fc=fc, tq=tq, dc=dc, slot=slot, bnk=bnk, b=b):
                            return E.matmul(
                                bank(bnk, QT),
                                AB.sl(o_wq + slot * DC * 128 + dc * 128, 128),
                                x1sl(dc, b * OB + tq * QT, QT),
                                start=(dc == 0), stop=(dc == DC - 1))
                        qm = em.op("tensor", mmq, waits=w_,
                                   inc=("t" if dc == DC - 1 else None))
                    wq_mark[fc] = ("t", qm)
                    def evQ(E, fc=fc, tq=tq, bnk=bnk):
                        return E.activation(
                            AB.sl(o_qct + fc * OB + tq * QT, QT), bank(bnk, QT),
                            AF.Identity, bias=sb_qkvb[:, fc:fc + 1], scale=1.0)
                    sm = em.op("scalar", evQ, waits=[("t", qm)], inc="s")
                    bank_w[bnk] = [("s", sm)]
            AB.pop()
            barrier()

            em.phase = f"attn{b}"
            slot_last_exp = {0: 0, 1: 0}
            P_hist = {(s, p): None for s in (0, 1) for p in (0, 1)}
            ctx_free = {}
            d_free = {}
            prev_last_ones = {0: 0, 1: 0}
            for j in range(NQT):
                M = QT * (j + 1) // 16
                NP = M // 2

                def c0f(m, M=M):
                    return max(0, 16 * (m - (M - NBAND)))

                for hp in range(H // 2):
                    heads = (2 * hp, 2 * hp + 1)
                    st = {}
                    for s in (0, 1):
                        st[s] = dict(pv_mark={}, pair_v={}, pair_g={}, ones_mark={},
                                     h=heads[s], cbank=4 + s, dbank=6 + s)

                    def emit_pv(s, g, j=j, M=M, st=st):
                        d = st[s]
                        mk = None
                        for ii, m in enumerate((2 * g, 2 * g + 1)):
                            first, last = (m == 0), (m == M - 1)
                            w_ = []
                            if ii == 0:
                                w_ = [("v", d["pair_v"][g])]
                                if first and ctx_free.get(s):
                                    w_.append(("v", ctx_free[s]))
                            cc = c0f(m)
                            def pv(E, m=m, g=g, cc=cc, first=first, last=last,
                                   s=s, d=d):
                                off = 512 if (m % 2) else cc
                                return E.matmul(
                                    PS[:, d["cbank"] * 512 + cc: d["cbank"] * 512 + QT],
                                    AB.sl(o_vKM + m * 128, 128),
                                    AB.sl(o_P + (2 * s + g % 2) * 1024 + off, QT - cc),
                                    start=first, stop=last)
                            mk = em.op("tensor", pv, waits=w_,
                                       inc=("t" if ii == 1 else None)) or mk
                        d["pv_mark"][g] = mk
                        P_hist[(s, g % 2)] = (mk, d["pair_v"][g], d["pair_g"][g])
                        return mk

                    def emit_ones(s, G, glast, st=st):
                        d = st[s]
                        ccg0, ccg1 = c0f(8 * G), c0f(8 * G + 1)
                        w_ = [("v", d["pair_g"][4 * G + 3])]
                        if G == 0 and d_free.get(s):
                            w_.append(("s", d_free[s]))
                        def o1(E, G=G, ccg0=ccg0, s=s, d=d):
                            return E.matmul(
                                PS[0:1, d["dbank"] * 512 + ccg0: d["dbank"] * 512 + QT],
                                ones_col, AB.sl(o_A0 + s * 512 + ccg0, QT - ccg0),
                                start=(G == 0), stop=False)
                        em.op("tensor", o1, waits=w_)
                        def o2(E, glast=glast, ccg1=ccg1, s=s, d=d):
                            return E.matmul(
                                PS[0:1, d["dbank"] * 512 + ccg1: d["dbank"] * 512 + QT],
                                ones_col, AB.sl(o_A1 + s * 512 + ccg1, QT - ccg1),
                                start=False, stop=glast)
                        mk = em.op("tensor", o2, inc="t")
                        d["ones_mark"][G] = mk
                        return mk

                    def emit_gstep(g):
                        m0, m1 = 2 * g, 2 * g + 1
                        cc0, cc1 = c0f(m0), c0f(m1)
                        emks = {}
                        smks = {}
                        for s in (0, 1):
                            d = st[s]
                            h = d["h"]
                            sslot = s
                            def mmS0(E, m0=m0, sslot=sslot, cc0=cc0, h=h, j=j):
                                return E.matmul(
                                    PS[:, sslot * 1024 + cc0: sslot * 1024 + QT],
                                    AB.sl(o_kT + m0 * 128, 128), qrhs(h, j, cc0),
                                    start=True, stop=True)
                            em.op("tensor", mmS0, waits=[("s", slot_last_exp[sslot])])
                            def mmS1(E, m1=m1, sslot=sslot, cc1=cc1, h=h, j=j):
                                return E.matmul(
                                    PS[:, sslot * 1024 + 512: sslot * 1024 + 512 + QT - cc1],
                                    AB.sl(o_kT + m1 * 128, 128), qrhs(h, j, cc1),
                                    start=True, stop=True)
                            smks[s] = em.op("tensor", mmS1, inc="t")
                        for s in (0, 1):
                            if g > 0:
                                emit_pv(s, g - 1)
                            if g % 4 == 0 and g > 0:
                                emit_ones(s, g // 4 - 1, glast=False)
                        for s in (0, 1):
                            smk = smks[s]
                            w_ = [("t", smk)]
                            ph = P_hist[(s, g % 2)]
                            if ph:
                                w_ += [("v", ph[2])]
                            def ex(E, s=s, g=g, cc0=cc0, cc1=cc1):
                                return E.activation(
                                    AB.sl(o_P + (2 * s + g % 2) * 1024 + cc0,
                                          512 + QT - cc1 - cc0),
                                    PS[:, s * 1024 + cc0: s * 1024 + 512 + QT - cc1],
                                    AF.Exp, scale=SCALE)
                            emk = em.op("scalar", ex, waits=w_, inc="s")
                            emks[s] = emk
                            slot_last_exp[s] = emk
                        first_of_group = (g % 4 == 0)
                        for s in (0, 1):
                            d = st[s]
                            vw = [("s", emks[s])]
                            if first_of_group:
                                po = d["ones_mark"].get(g // 4 - 1,
                                                        prev_last_ones[s])
                                if po:
                                    vw.append(("t", po))
                            for ii, m, cc in ((0, m0, cc0), (1, m1, cc1)):
                                if m >= M - NBAND:
                                    def bd(E, ii=ii, g=g, cc=cc, s=s):
                                        off = 512 if ii else cc
                                        pp = AB.sl(o_P + (2 * s + g % 2) * 1024 + off, 16)
                                        return E.tensor_tensor(pp, pp, sb_stair[:, :],
                                                               AL.mult)
                                    em.op("vector", bd, waits=vw)
                                    vw = []
                            def a0(E, g=g, cc=cc0, s=s, cp=first_of_group):
                                dst = AB.sl(o_A0 + s * 512 + cc, QT - cc)
                                srcp = AB.sl(o_P + (2 * s + g % 2) * 1024 + cc, QT - cc)
                                if cp:
                                    return E.tensor_copy(dst, srcp)
                                return E.tensor_tensor(dst, dst, srcp, AL.add)
                            vm = em.op("vector", a0, waits=vw, inc="v")
                            d["pair_v"][g] = vm
                        for s in (0, 1):
                            d = st[s]
                            def a1(E, g=g, cc=cc1, s=s, cp=first_of_group):
                                dst = AB.sl(o_A1 + s * 512 + cc, QT - cc)
                                srcp = AB.sl(o_P + (2 * s + g % 2) * 1024 + 512, QT - cc)
                                if cp:
                                    return E.tensor_copy(dst, srcp)
                                return E.tensor_tensor(dst, dst, srcp, AL.add)
                            d["pair_g"][g] = em.op("vector", a1, inc="v")

                    for g in range(NP):
                        emit_gstep(g)
                    lastpv = {}
                    for s in (0, 1):
                        lastpv[s] = emit_pv(s, NP - 1)
                        prev_last_ones[s] = emit_ones(s, M // 8 - 1, glast=True)
                    sm1 = {}
                    for s in (0, 1):
                        def drow(E, s=s, d=st[s]):
                            return E.copy(AB.sl(o_dr + s * 512, QT, 0, 1),
                                          PS[0:1, d["dbank"] * 512: d["dbank"] * 512 + QT])
                        sm1[s] = em.op("scalar", drow,
                                       waits=[("t", prev_last_ones[s])], inc="s")
                    tm1 = {}
                    for s in (0, 1):
                        def bcast(E, s=s, d=st[s]):
                            return E.matmul(bank(d["dbank"], QT), ones_row,
                                            AB.sl(o_dr + s * 512, QT, 0, 1),
                                            start=True, stop=True)
                        tm1[s] = em.op("tensor", bcast, waits=[("s", sm1[s])], inc="t")
                    sm2 = {}
                    for s in (0, 1):
                        def bcp(E, s=s, d=st[s]):
                            return E.copy(FA.sl(o_Bsb + s * 512, QT), bank(d["dbank"], QT))
                        sm2[s] = em.op("scalar", bcp, waits=[("t", tm1[s])], inc="s")
                        d_free[s] = sm2[s]
                    rcpm = {}
                    for s in (0, 1):
                        def rcp(E, s=s):
                            return E.reciprocal(FA.sl(o_Rsb + s * 512, QT),
                                                FA.sl(o_Bsb + s * 512, QT))
                        rcpm[s] = em.op("vector", rcp, waits=[("s", sm2[s])], inc="v")
                    for s in (0, 1):
                        def dvd(E, s=s, j=j, d=st[s]):
                            return E.tensor_tensor(
                                AB.sl(o_qct + d["h"] * OB + j * QT, QT),
                                bank(d["cbank"], QT),
                                FA.sl(o_Rsb + s * 512, QT), AL.mult)
                        ctx_free[s] = em.op("vector", dvd,
                                            waits=[("t", lastpv[s]), ("v", rcpm[s])],
                                            inc="v")
            barrier()
            FA.pop()
            AB.pop()

            em.phase = f"outln{b}"
            AB.push()
            o_s1 = AB.alloc(DC * OB)
            o_sq = AB.alloc(2 * 512)
            o_ow = AB.alloc(2 * DC * 128)
            o_mB = AB.alloc(2 * 512)
            o_rB = AB.alloc(2 * 512)
            o_tc = AB.alloc(2 * 512)
            o_mr = AB.alloc(2 * 512)
            o_rr = AB.alloc(2 * 512)
            FA.push()
            o_xr = FA.alloc(4 * 512)
            o_t1 = FA.alloc(2 * 512)
            o_rw = FA.alloc(2 * 1024)
            o_scr1 = FA.alloc(2 * 512)

            def s1sl(fc, tq):
                return AB.sl(o_s1 + (fc * NQT + tq) * QT, QT)

            ow_mark, xr_mark, s1_mark, sq_t = {}, {}, {}, {}
            msq_mark = {}
            stat_last = None
            for fc in range(DC):
                slot = fc % 2
                w_ = [ow_mark[fc - 2]] if fc >= 2 else []
                def dow(E, fc=fc, slot=slot):
                    return E.dma_start(
                        AB.sl(o_ow + slot * DC * 128, DC * 128).rearrange(
                            "p (c f) -> p c f", c=DC),
                        outwT[:, fc * 128:(fc + 1) * 128].rearrange(
                            "(c p) f -> p c f", p=128))
                dm = em.op("gpsimd", dow, waits=w_, inc="gd", amt=16)
                for tq in range(NQT):
                    i_ = fc * NQT + tq
                    xslot = i_ % 4
                    w_ = [xr_mark[i_ - 4]] if i_ >= 4 else []
                    def dxr(E, fc=fc, tq=tq, xslot=xslot, b=b):
                        return E.dma_start(
                            FA.sl(o_xr + xslot * 512, QT),
                            xTown[fc * 128:(fc + 1) * 128,
                                  b * OB + tq * QT:b * OB + (tq + 1) * QT])
                    xdm = em.op("sync", dxr, waits=w_, inc="sd", amt=16)
                    bnk = i_ % 2
                    for dc in range(DC):
                        w_ = ([("gd", dm)] + bank_w[bnk]) if dc == 0 else []
                        def mmo(E, dc=dc, tq=tq, slot=slot, bnk=bnk):
                            return E.matmul(
                                bank(bnk, QT),
                                AB.sl(o_ow + slot * DC * 128 + dc * 128, 128),
                                AB.sl(o_qct + dc * OB + tq * QT, QT),
                                start=(dc == 0), stop=(dc == DC - 1))
                        om = em.op("tensor", mmo, waits=w_,
                                   inc=("t" if dc == DC - 1 else None))
                    ow_mark[fc] = ("t", om)
                    tslot = i_ % 2
                    w_ = [("t", om), ("sd", xdm)]
                    if i_ >= 2:
                        w_.append(("s", s1_mark[i_ - 2]))
                    def ev1(E, bnk=bnk, xslot=xslot, tslot=tslot):
                        return E.tensor_tensor(
                            FA.sl(o_t1 + tslot * 512, QT), bank(bnk, QT),
                            FA.sl(o_xr + xslot * 512, QT), AL.add)
                    vm = em.op("vector", ev1, waits=w_, inc="v")
                    xr_mark[i_] = ("v", vm)
                    bank_w[bnk] = [("v", vm)]
                    def ev2(E, fc=fc, tq=tq, tslot=tslot):
                        return E.activation(
                            s1sl(fc, tq), FA.sl(o_t1 + tslot * 512, QT),
                            AF.Identity, bias=sb_outb[:, fc:fc + 1], scale=1.0)
                    s1_mark[i_] = em.op("scalar", ev2, waits=[("v", vm)], inc="s")
                if fc >= 1:
                    for tq in range(NQT):
                        w_ = [msq_mark[(fc - 2, tq)]] if (fc - 2, tq) in msq_mark else []
                        def sq_(E, fc=fc, tq=tq):
                            return E.activation(AB.sl(o_sq + tq * 512, QT),
                                                s1sl(fc - 1, tq), AF.Square, scale=1.0)
                        smq = em.op("scalar", sq_, waits=w_, inc="s")
                        stb, sqb = 2 + tq, 2 + NQT + tq
                        def mst(E, fc=fc, tq=tq, stb=stb):
                            return E.matmul(bank(stb, QT, rows=(0, 1)), ones_col,
                                            s1sl(fc - 1, tq),
                                            start=(fc == 1), stop=False)
                        em.op("tensor", mst,
                              waits=[("s", s1_mark[(fc - 1) * NQT + tq])] +
                                    (bank_w[stb] if fc == 1 else []))
                        def msq(E, fc=fc, tq=tq, sqb=sqb):
                            return E.matmul(bank(sqb, QT, rows=(0, 1)), ones_col,
                                            AB.sl(o_sq + tq * 512, QT),
                                            start=(fc == 1), stop=False)
                        msq_mark[(fc - 1, tq)] = ("t", em.op(
                            "tensor", msq,
                            waits=[("s", smq)] + (bank_w[sqb] if fc == 1 else []),
                            inc="t"))
            for tq in range(NQT):
                w_ = [msq_mark[(DC - 2, tq)]] if (DC - 2, tq) in msq_mark else []
                def sq_(E, tq=tq):
                    return E.activation(AB.sl(o_sq + tq * 512, QT),
                                        s1sl(DC - 1, tq), AF.Square, scale=1.0)
                smq = em.op("scalar", sq_, waits=w_, inc="s")
                stb, sqb = 2 + tq, 2 + NQT + tq
                def mst(E, tq=tq, stb=stb):
                    return E.matmul(bank(stb, QT, rows=(0, 1)), ones_col,
                                    s1sl(DC - 1, tq), start=(DC == 1), stop=True)
                em.op("tensor", mst, waits=[("s", s1_mark[(DC - 1) * NQT + tq])])
                def msq(E, tq=tq, sqb=sqb):
                    return E.matmul(bank(sqb, QT, rows=(0, 1)), ones_col,
                                    AB.sl(o_sq + tq * 512, QT), start=(DC == 1), stop=True)
                stat_last = em.op("tensor", msq, waits=[("s", smq)], inc="t")

            cpk1 = ln_rows(NQT, QT, o_rw, o_scr1, stat_last,
                           lambda t: 2 + t, lambda t: 2 + NQT + t, o_mr, o_rr)
            s1_bc = {}
            for tq in range(NQT):
                bb_ = 6 + (tq % 2)
                def bm1(E, tq=tq, bb_=bb_):
                    return E.matmul(bank(bb_, QT), ones_row,
                                    AB.sl(o_mr + tq * 512, QT, 0, 1),
                                    start=True, stop=True)
                tb1 = em.op("tensor", bm1, waits=[("v", cpk1[tq])] + bank_w[bb_], inc="t")
                def cb1(E, tq=tq, bb_=bb_):
                    return E.copy(AB.sl(o_mB + tq * 512, QT), bank(bb_, QT))
                sb1 = em.op("scalar", cb1, waits=[("t", tb1)], inc="s")
                def bm2(E, tq=tq, bb_=bb_):
                    return E.matmul(bank(bb_, QT), ones_row,
                                    AB.sl(o_rr + tq * 512, QT, 0, 1),
                                    start=True, stop=True)
                tb2 = em.op("tensor", bm2, waits=[("s", sb1)], inc="t")
                def cb2(E, tq=tq, bb_=bb_):
                    return E.copy(AB.sl(o_rB + tq * 512, QT), bank(bb_, QT))
                sb2 = em.op("scalar", cb2, waits=[("t", tb2)], inc="s")
                bank_w[bb_] = [("s", sb2)]
                s1_mark[("bc", tq)] = sb2
            pend = []
            for fc in range(DC):
                for tq in range(NQT):
                    tslot = (fc * NQT + tq) % 2
                    w_ = [("s", s1_mark[("bc", tq)])] if fc == 0 else []
                    def fsub(E, fc=fc, tq=tq, tslot=tslot):
                        return E.tensor_tensor(
                            AB.sl(o_tc + tslot * 512, QT), s1sl(fc, tq),
                            AB.sl(o_mB + tq * 512, QT), AL.subtract)
                    em.op("vector", fsub, waits=w_)
                    pend.append((fc, tq, tslot))
                    if len(pend) == 2:
                        for fc_, tq_, ts_ in pend:
                            def fmul(E, fc_=fc_, tq_=tq_, ts_=ts_, b=b):
                                return E.tensor_tensor(
                                    x1sl(fc_, b * OB + tq_ * QT, QT),
                                    AB.sl(o_tc + ts_ * 512, QT),
                                    AB.sl(o_rB + tq_ * 512, QT), AL.mult)
                            em.op("vector", fmul, inc="v")
                        pend = []
            for fc_, tq_, ts_ in pend:
                def fmul(E, fc_=fc_, tq_=tq_, ts_=ts_, b=b):
                    return E.tensor_tensor(
                        x1sl(fc_, b * OB + tq_ * QT, QT),
                        AB.sl(o_tc + ts_ * 512, QT),
                        AB.sl(o_rB + tq_ * 512, QT), AL.mult)
                em.op("vector", fmul, inc="v")
            if DEBUG_DUMP and b == 0:
                em.op("gpsimd", lambda E: E.dma_start(
                    dbg_ctx[:, :], AB.sl(o_qct, H * OB)),
                    waits=[("v", C["v"]), ("s", C["s"]), ("t", C["t"])],
                    inc="gd", amt=16)
            FA.pop()
            AB.pop()
            AB.pop()
            barrier()

        em.phase = "router"
        FA.push()
        o_ssb = FA.alloc(2 * 512)
        o_gx = FA.alloc(4 * 64)
        gate_done = 0
        for tt_i in range(NTS):
            sslot = tt_i % 2
            for dc in range(DC):
                w_ = bank_w[0] if dc == 0 else []
                def mr(E, tt_i=tt_i, dc=dc):
                    return E.matmul(
                        bank(0, TSW, rows=(0, 8)), AB.sl(o_rtr + dc * NE, NE),
                        AB.sl(o_x1 + dc * TOWN + tt_i * TSW, TSW),
                        start=(dc == 0), stop=(dc == DC - 1))
                rm = em.op("tensor", mr, waits=w_, inc=("t" if dc == DC - 1 else None))
            def cps(E, sslot=sslot):
                return E.copy(FA.sl(o_ssb + sslot * 512, TSW, 0, 8),
                              bank(0, TSW, rows=(0, 8)))
            sms = em.op("scalar", cps, waits=[("t", rm)], inc="s")
            bank_w[0] = [("s", sms)]
            NG = TSW // 128
            trm = {}
            for gg in range(NG):
                tb = 1 + gg % 2
                def tr(E, gg=gg, sslot=sslot, tb=tb):
                    return E.transpose(bank(tb, 8),
                                       FA.sl(o_ssb + sslot * 512 + gg * 128, 128, 0, 8),
                                       sb_identf[0:8, 0:8])
                trm[gg] = em.op("tensor", tr, waits=[("s", sms)] + bank_w[tb], inc="t")
                bank_w[tb] = []

            def gx(gg, k):
                return FA.sl(o_gx + gg * 64 + k * 8, 8)

            def gmr(gg, k):
                return FA.sl(o_gx + gg * 64 + 48 + k, 1)

            for gg in range(NG):
                def f(E, gg=gg, tb=1 + gg % 2):
                    return E.tensor_copy(gx(gg, 0), bank(tb, 8))
                vmc = em.op("vector", f, waits=[("t", trm[gg])], inc="v")
                bank_w[1 + gg % 2] = [("v", vmc)]
            for gg in range(NG):
                em.op("vector", lambda E, gg=gg: E.reduce_max(gmr(gg, 0), gx(gg, 0), axis=AX.X))
            for gg in range(NG):
                em.op("vector", lambda E, gg=gg: E.tensor_scalar(
                    gx(gg, 1), gx(gg, 0), gmr(gg, 0), None, AL.is_equal))
            for gg in range(NG):
                em.op("vector", lambda E, gg=gg: E.tensor_scalar(
                    gx(gg, 1), gx(gg, 1), sb_cst[:, K_NEG30:K_NEG30 + 1], None, AL.mult))
            for gg in range(NG):
                em.op("vector", lambda E, gg=gg: E.tensor_tensor(
                    gx(gg, 1), gx(gg, 1), gx(gg, 0), AL.add))
            for gg in range(NG):
                em.op("vector", lambda E, gg=gg: E.reduce_max(gmr(gg, 1), gx(gg, 1), axis=AX.X))
            for gg in range(NG):
                em.op("vector", lambda E, gg=gg: E.tensor_scalar(
                    gx(gg, 2), gx(gg, 0), gmr(gg, 1), None, AL.is_ge))
            vmn = None
            for gg in range(NG):
                vmn = em.op("vector", lambda E, gg=gg: E.tensor_scalar(
                    gmr(gg, 2), gmr(gg, 0), -1.0, None, AL.mult), inc="v")
            for gg in range(NG):
                em.op("scalar", lambda E, gg=gg: E.activation(
                    gx(gg, 3), gx(gg, 0), AF.Exp, bias=gmr(gg, 2), scale=1.0),
                    waits=([("v", vmn)] if gg == 0 else []))
            sme = None
            for gg in range(NG):
                sme = em.op("scalar", lambda E, gg=gg: E.activation(
                    gmr(gg, 3), gmr(gg, 1), AF.Exp, bias=gmr(gg, 2), scale=1.0), inc="s")
            for gg in range(NG):
                em.op("vector", lambda E, gg=gg: E.tensor_scalar(
                    gmr(gg, 4), gmr(gg, 3), sb_cst[:, K_ONE:K_ONE + 1], None, AL.add),
                    waits=([("s", sme)] if gg == 0 else []))
            for gg in range(NG):
                em.op("vector", lambda E, gg=gg: E.reciprocal(gmr(gg, 5), gmr(gg, 4)))
            for gg in range(NG):
                em.op("vector", lambda E, gg=gg: E.tensor_scalar(
                    gx(gg, 4), gx(gg, 3), gmr(gg, 5), None, AL.mult))
            vmg = None
            for gg in range(NG):
                vmg = em.op("vector", lambda E, gg=gg: E.tensor_tensor(
                    gx(gg, 4), gx(gg, 4), gx(gg, 2), AL.mult), inc="v")
            for gg in range(NG):
                tb = 3 + gg % 2
                def trb(E, gg=gg, tb=tb):
                    return E.transpose(bank(tb, 128, rows=(0, 8)), gx(gg, 4),
                                      sb_identf[:, :])
                tmb = em.op("tensor", trb, waits=[("v", vmg)] + bank_w[tb], inc="t")
                def cpg(E, gg=gg, tt_i=tt_i, tb=tb):
                    return E.copy(AB.sl(o_gates + tt_i * TSW + gg * 128, 128, 0, 8),
                                  bank(tb, 128, rows=(0, 8)))
                gate_done = em.op("scalar", cpg, waits=[("t", tmb)], inc="s")
                bank_w[tb] = [("s", gate_done)]
        FA.pop()
        barrier()

        em.phase = "ffn"
        AB.push()
        o_y = AB.alloc(DC * TOWN)
        AB.push()
        o_h = AB.alloc(FFC * TOWN)
        o_wf = AB.alloc(2 * 2 * DC * 128)
        o_w2 = AB.alloc(2 * FFC * 128)
        o_gB = AB.alloc(TOWN)
        o_sil = AB.alloc(2 * 512)
        o_t2 = AB.alloc(2 * 512)

        wf_mark, w2_mark = {}, {}
        w2_last_prev = 0
        NTSG = (NTS + 1) // 2
        for e_i in range(NE):
            gB_mark = {}
            vprev = C["v"]
            for ts in range(NTS):
                gb_ = ts % 2
                def gbm(E, e_i=e_i, ts=ts, gb_=gb_):
                    return E.matmul(bank(gb_, TSW), AB.sl(o_esel + e_i * 128, 128, 0, 8),
                                    AB.sl(o_gates + ts * TSW, TSW, 0, 8),
                                    start=True, stop=True)
                tmg = em.op("tensor", gbm,
                            waits=[("s", gate_done)] + bank_w[gb_], inc="t")
                def gbc(E, ts=ts, gb_=gb_):
                    return E.copy(AB.sl(o_gB + ts * TSW, TSW), bank(gb_, TSW))
                smg = em.op("scalar", gbc,
                            waits=[("t", tmg), ("v", vprev)], inc="s")
                bank_w[gb_] = [("s", smg)]
                gB_mark[ts] = smg
            h_rd = {}
            first_of_expert = True
            for ffc in range(FFC):
                slot = ffc % 2
                w_ = []
                key = ("wf", e_i, ffc - 2)
                if key in wf_mark:
                    w_.append(wf_mark[key])
                elif e_i > 0:
                    key = ("wf", e_i - 1, ffc + FFC - 2)
                    if key in wf_mark:
                        w_.append(wf_mark[key])
                def dw1(E, e_i=e_i, ffc=ffc, slot=slot):
                    return E.dma_start(
                        AB.sl(o_wf + (slot * 2 + 0) * DC * 128, DC * 128).rearrange(
                            "p (c f) -> p c f", c=DC),
                        w1[e_i][:, ffc * 128:(ffc + 1) * 128].rearrange(
                            "(c p) f -> p c f", p=128))
                em.op("gpsimd", dw1, waits=w_, inc="gd", amt=16)
                def dw3(E, e_i=e_i, ffc=ffc, slot=slot):
                    return E.dma_start(
                        AB.sl(o_wf + (slot * 2 + 1) * DC * 128, DC * 128).rearrange(
                            "p (c f) -> p c f", c=DC),
                        w3[e_i][:, ffc * 128:(ffc + 1) * 128].rearrange(
                            "(c p) f -> p c f", p=128))
                dmw = em.op("gpsimd", dw3, inc="gd", amt=16)
                for tsg in range(NTSG):
                    par = (ffc * NTSG + tsg) % 2
                    tss = [tsg * 2 + k for k in range(2) if tsg * 2 + k < NTS]
                    bks = {}
                    for k, ts in enumerate(tss):
                        bks[(0, ts)] = 4 * par + k
                        bks[(1, ts)] = 4 * par + 2 + k
                    lastmm = None
                    hjobs = [(dc, w_i, k, ts)
                             for dc in range(DC) for w_i in (0, 1)
                             for k, ts in enumerate(tss)]
                    for dc, w_i, k, ts in hjobs:
                        bnk = bks[(w_i, ts)]
                        w_ = []
                        if dc == 0 and w_i == 0 and k == 0:
                            w_ = [("gd", dmw)]
                            for bq in sorted(set(bks.values())):
                                w_ += bank_w[bq]
                        def mmh(E, e_i=e_i, ffc=ffc, slot=slot, w_i=w_i,
                                ts=ts, dc=dc, bnk=bnk):
                            return E.matmul(
                                bank(bnk, TSW),
                                AB.sl(o_wf + (slot * 2 + w_i) * DC * 128 + dc * 128, 128),
                                AB.sl(o_x1 + dc * TOWN + ts * TSW, TSW),
                                start=(dc == 0), stop=(dc == DC - 1))
                        inc = "t" if (dc == DC - 1 and w_i == 1 and
                                      k == len(tss) - 1) else None
                        lastmm = em.op("tensor", mmh, waits=w_, inc=inc) or lastmm
                    wf_mark[("wf", e_i, ffc)] = ("t", lastmm)
                    silm = {}
                    prev_vmh = C["v"]
                    for k, ts in enumerate(tss):
                        w_ = [("t", lastmm), ("v", prev_vmh)]
                        def sil(E, ts=ts, k=k, bks=bks):
                            return E.activation(AB.sl(o_sil + k * 512, TSW),
                                                bank(bks[(0, ts)], TSW), AF.Silu,
                                                scale=1.0)
                        silm[ts] = em.op("scalar", sil, waits=w_, inc="s")
                    for k, ts in enumerate(tss):
                        w_ = [("s", silm[ts])]
                        if first_of_expert and k == 0:
                            w_.append(("s", gB_mark[tss[-1]]))
                            if w2_last_prev:
                                w_.append(("t", w2_last_prev))
                        def mu1(E, ts=ts, k=k, bks=bks):
                            return E.tensor_tensor(
                                AB.sl(o_t2 + k * 512, TSW), AB.sl(o_sil + k * 512, TSW),
                                bank(bks[(1, ts)], TSW), AL.mult)
                        em.op("vector", mu1, waits=w_)
                    first_of_expert = False
                    vmh = None
                    for k, ts in enumerate(tss):
                        def mu2(E, ts=ts, k=k, ffc=ffc):
                            return E.tensor_tensor(
                                AB.sl(o_h + ffc * TOWN + ts * TSW, TSW),
                                AB.sl(o_t2 + k * 512, TSW),
                                AB.sl(o_gB + ts * TSW, TSW), AL.mult)
                        vmh = em.op("vector", mu2, inc="v")
                        h_rd[(ffc, ts)] = vmh
                    for bq in sorted(set(bks.values())):
                        bank_w[bq] = [("v", vmh)]
            for fc in range(DC):
                slot = fc % 2
                w_ = [w2_mark[("w2", e_i, fc - 2)]] if ("w2", e_i, fc - 2) in w2_mark else []
                def dw2(E, e_i=e_i, fc=fc, slot=slot):
                    return E.dma_start(
                        AB.sl(o_w2 + slot * FFC * 128, FFC * 128).rearrange(
                            "p (c f) -> p c f", c=FFC),
                        w2[e_i][:, fc * 128:(fc + 1) * 128].rearrange(
                            "(c p) f -> p c f", p=128))
                dm2 = em.op("gpsimd", dw2, waits=w_, inc="gd", amt=16)
                for ts in range(NTS):
                    bnk = (fc * NTS + ts) % 4
                    for ffc in range(FFC):
                        w_ = []
                        if ffc == 0:
                            w_ = [("gd", dm2)] + bank_w[bnk]
                            w_ += [("v", h_rd[(fq, ts)]) for fq in range(FFC)]
                        def mmy(E, fc=fc, ts=ts, ffc=ffc, slot=slot, bnk=bnk):
                            return E.matmul(
                                bank(bnk, TSW),
                                AB.sl(o_w2 + slot * FFC * 128 + ffc * 128, 128),
                                AB.sl(o_h + ffc * TOWN + ts * TSW, TSW),
                                start=(ffc == 0), stop=(ffc == FFC - 1))
                        ym = em.op("tensor", mmy, waits=w_,
                                   inc=("t" if ffc == FFC - 1 else None))
                    w2_mark[("w2", e_i, fc)] = ("t", ym)
                    w2_last_prev = ym
                    def yac(E, e_i=e_i, fc=fc, ts=ts, bnk=bnk):
                        ysl = AB.sl(o_y + fc * TOWN + ts * TSW, TSW)
                        if e_i == 0:
                            return E.tensor_copy(ysl, bank(bnk, TSW))
                        return E.tensor_tensor(ysl, ysl, bank(bnk, TSW), AL.add)
                    vmy = em.op("vector", yac, waits=[("t", ym)], inc="v")
                    bank_w[bnk] = [("v", vmy)]
        barrier()
        AB.pop()

        em.phase = "ln2"
        s2m = None
        for fc in range(DC):
            def s2a(E, fc=fc):
                return E.tensor_tensor(AB.sl(o_y + fc * TOWN, TOWN),
                                       AB.sl(o_y + fc * TOWN, TOWN),
                                       AB.sl(o_x1 + fc * TOWN, TOWN), AL.add)
            s2m = em.op("vector", s2a, inc="v")
        AB.push()
        o_sq2 = AB.alloc(NTS * 512)
        o_mB2 = AB.alloc(2 * 512)
        o_rB2 = AB.alloc(2 * 512)
        o_tc2 = AB.alloc(2 * 512)
        o_mr2 = AB.alloc(NTS * 512)
        o_rr2 = AB.alloc(NTS * 512)
        FA.push()
        o_rw2 = FA.alloc(NTS * 1024)
        o_ot = FA.alloc(2 * 512)
        o_scr2 = FA.alloc(2 * 512)
        msq2_mark = {}
        stat2_last = None
        for fc in range(DC):
            for ts in range(NTS):
                w_ = [("v", s2m)] if fc == 0 else []
                if (fc - 1, ts) in msq2_mark:
                    w_.append(msq2_mark[(fc - 1, ts)])
                def sq2(E, fc=fc, ts=ts):
                    return E.activation(AB.sl(o_sq2 + ts * 512, TSW),
                                        AB.sl(o_y + fc * TOWN + ts * TSW, TSW),
                                        AF.Square, scale=1.0)
                smq = em.op("scalar", sq2, waits=w_, inc="s")
                stb, sqb = ts % 4, 4 + ts % 4
                def mst2(E, fc=fc, ts=ts, stb=stb):
                    return E.matmul(bank(stb, TSW, rows=(0, 1)), ones_col,
                                    AB.sl(o_y + fc * TOWN + ts * TSW, TSW),
                                    start=(fc == 0), stop=(fc == DC - 1))
                em.op("tensor", mst2,
                      waits=([("v", s2m)] + bank_w[stb]) if fc == 0 else [])
                def msq2(E, fc=fc, ts=ts, sqb=sqb):
                    return E.matmul(bank(sqb, TSW, rows=(0, 1)), ones_col,
                                    AB.sl(o_sq2 + ts * 512, TSW),
                                    start=(fc == 0), stop=(fc == DC - 1))
                tms = em.op("tensor", msq2,
                            waits=[("s", smq)] + (bank_w[sqb] if fc == 0 else []),
                            inc="t")
                msq2_mark[(fc, ts)] = ("t", tms)
                stat2_last = tms
        cpk2 = ln_rows(NTS, TSW, o_rw2, o_scr2, stat2_last,
                       lambda t: t % 4, lambda t: 4 + t % 4, o_mr2, o_rr2)
        rowv = cpk2
        apply_v = {}
        ot_mark = {}
        for ts in range(NTS):
            rs = ts % 2
            bb1, bb2 = rs, 2 + rs
            w_ = [("v", rowv[ts])] + bank_w[bb1]
            if ts >= 2:
                w_.append(("v", apply_v[ts - 2]))
            def bm1(E, ts=ts, rs=rs, bb1=bb1):
                return E.matmul(bank(bb1, TSW), ones_row,
                                AB.sl(o_mr2 + ts * 512, TSW, 0, 1), start=True, stop=True)
            tb1 = em.op("tensor", bm1, waits=w_, inc="t")
            def cb1(E, ts=ts, rs=rs, bb1=bb1):
                return E.copy(AB.sl(o_mB2 + rs * 512, TSW), bank(bb1, TSW))
            sb1 = em.op("scalar", cb1, waits=[("t", tb1)], inc="s")
            bank_w[bb1] = [("s", sb1)]
            def bm2(E, ts=ts, rs=rs, bb2=bb2):
                return E.matmul(bank(bb2, TSW), ones_row,
                                AB.sl(o_rr2 + ts * 512, TSW, 0, 1), start=True, stop=True)
            w2_ = [("v", rowv[ts])] + bank_w[bb2]
            if ts >= 2:
                w2_.append(("v", apply_v[ts - 2]))
            tb2 = em.op("tensor", bm2, waits=w2_, inc="t")
            def cb2(E, ts=ts, rs=rs, bb2=bb2):
                return E.copy(AB.sl(o_rB2 + rs * 512, TSW), bank(bb2, TSW))
            sb2 = em.op("scalar", cb2, waits=[("t", tb2)], inc="s")
            bank_w[bb2] = [("s", sb2)]
            pend = []
            lastv = None
            for fc in range(DC):
                tsl = fc % 2
                w_ = [("s", sb2)] if fc == 0 else []
                def fs(E, fc=fc, ts=ts, tsl=tsl, rs=rs):
                    return E.tensor_tensor(
                        AB.sl(o_tc2 + tsl * 512, TSW),
                        AB.sl(o_y + fc * TOWN + ts * TSW, TSW),
                        AB.sl(o_mB2 + rs * 512, TSW), AL.subtract)
                em.op("vector", fs, waits=w_)
                pend.append((fc, tsl))
                if len(pend) == 2:
                    for fc_, tsl_ in pend:
                        osl = fc_ % 2
                        w_ = []
                        k1, k2 = (fc_ - 2, ts), (fc_ - 2 + DC, ts - 1)
                        if k1 in ot_mark:
                            w_.append(ot_mark[k1])
                        elif k2 in ot_mark:
                            w_.append(ot_mark[k2])
                        def fm(E, fc_=fc_, ts=ts, tsl_=tsl_, osl=osl, rs=rs):
                            return E.tensor_tensor(
                                FA.sl(o_ot + osl * 512, TSW),
                                AB.sl(o_tc2 + tsl_ * 512, TSW),
                                AB.sl(o_rB2 + rs * 512, TSW), AL.mult)
                        lastv = em.op("vector", fm, waits=w_, inc="v")
                        def dout(E, fc_=fc_, ts=ts, osl=osl):
                            return E.dma_start(
                                out[fc_ * 128:(fc_ + 1) * 128, ts * TSW:(ts + 1) * TSW],
                                FA.sl(o_ot + osl * 512, TSW))
                        dmo = em.op("sync", dout, waits=[("v", lastv)], inc="sd", amt=16)
                        ot_mark[(fc_, ts)] = ("sd", dmo)
                    pend = []
            apply_v[ts] = lastv
        FA.pop()
        AB.pop()
        AB.pop()
        if DEBUG_DUMP:
            dm1 = em.op("gpsimd", lambda E: E.dma_start(
                dbg_x1[:, :], AB.sl(o_x1, DC * TOWN)),
                waits=[("v", C["v"]), ("s", C["s"]), ("t", C["t"])], inc="gd", amt=16)
            dm2 = em.op("gpsimd", lambda E: E.dma_start(
                dbg_g[:, :], AB.sl(o_gates, TOWN, 0, 8)), inc="gd", amt=16)
            em.op("gpsimd", None, waits=[("gd", dm2)])
        em.op("sync", None, waits=[("sd", C["sd"])])

        em.check()

        def runner(eng):
            def f(E):
                cur, sid = None, None
                for w, fn, inc, amt, ph in em.prog[eng]:
                    if USE_SCOPES and ph != cur:
                        if cur is not None:
                            nc.leave_named_scope(cur, sid, False)
                        sid = nc.enter_named_scope(ph, False)[0]
                        cur = ph
                    for sn, val in w:
                        E.wait_ge(sems[sn], val)
                    inst = fn(E) if fn is not None else None
                    if inc is not None:
                        if inst is None:
                            inst = E.nop()
                        inst.then_inc(sems[inc], amt)
                if USE_SCOPES and cur is not None:
                    nc.leave_named_scope(cur, sid, False)
            return f
        for e in ENG:
            getattr(blk, e)(runner(e))
    return nc, dv


def _prep_inputs(cfg, inputs):
    dv = _derive(cfg)
    B, D, E = cfg["B"], cfg["D"], cfg["E"]
    DC, OB = dv["DC"], dv["OB"]
    x = np.asarray(inputs["L0_attn_x"], np.float32)
    xT = np.ascontiguousarray(x.transpose(0, 2, 1))
    wqkvT = np.ascontiguousarray(np.asarray(inputs["L0_attn_Wqkv_w"], np.float32).T)
    wqkvb = np.ascontiguousarray(
        np.asarray(inputs["L0_attn_Wqkv_b"], np.float32).reshape(-1, 128).T)
    outwT = np.ascontiguousarray(np.asarray(inputs["L0_attn_out_w"], np.float32).T)
    outb = np.ascontiguousarray(
        np.asarray(inputs["L0_attn_out_b"], np.float32).reshape(DC, 128).T)
    routerT = np.ascontiguousarray(np.asarray(inputs["L0_ffn_router_w"], np.float32).T)
    w1 = np.ascontiguousarray(np.asarray(inputs["L0_ffn_w1"], np.float32))
    w3 = np.ascontiguousarray(np.asarray(inputs["L0_ffn_w3"], np.float32))
    w2 = np.ascontiguousarray(np.asarray(inputs["L0_ffn_w2"], np.float32))
    identf = np.eye(128, dtype=np.float32)
    onesbf = np.ones((128, 128), dtype=np.float32)
    esel = np.zeros((8, E * 128), dtype=np.float32)
    for e_i in range(E):
        esel[e_i, e_i * 128:(e_i + 1) * 128] = 1.0
    cst = np.zeros((128, 8), dtype=np.float32)
    cst[:, 0] = 1.0
    cst[:, 1] = 1e-5
    cst[:, 2] = 1.0 / D
    cst[:, 3] = -1e30
    maps = []
    for c in range(NCORES):
        xTown = np.ascontiguousarray(
            np.concatenate([xT[b][:, c::NCORES] for b in range(B)], axis=1))
        r = np.arange(128).reshape(128, 1)
        ii = np.arange(16).reshape(1, 16)
        stair = (r <= 8 * ii + c).astype(np.float32)
        maps.append({
            "xT": xT, "xTown": xTown, "wqkvT": wqkvT, "wqkvb": wqkvb,
            "outwT": outwT, "outb": outb, "routerT": routerT,
            "w1": w1, "w3": w3, "w2": w2, "identf": identf,
            "onesbf": onesbf, "stairbf": stair, "eselbf": esel, "cstf": cst,
        })
    return dv, maps


def kernel(**inputs):
    cfg = dict(CFG)
    dv, maps = _prep_inputs(cfg, inputs)
    nc, _ = build(cfg)
    res = run_bass_kernel_spmd(nc, maps, core_ids=list(range(NCORES)))
    B, S, D = cfg["B"], cfg["S"], cfg["D"]
    OB = dv["OB"]
    outx = np.empty((B, S, D), dtype=np.float32)
    for c in range(NCORES):
        oc = np.asarray(res.results[c]["out"])
        for b in range(B):
            outx[b, c::NCORES, :] = oc[:, b * OB:(b + 1) * OB].T
    return outx
